# version 04
# baseline (speedup 1.0000x reference)
import numpy as np
from contextlib import ExitStack

import concourse.bass as bass
import concourse.mybir as mybir
from concourse.bass_utils import run_bass_kernel_spmd

F32 = mybir.dt.float32
BF16 = mybir.dt.bfloat16
AF = mybir.ActivationFunctionType
ALU = mybir.AluOpType
AX = mybir.AxisListType

NCORES = 8
T_CORE = 2048
NT = 4
TT = 512
HID = 4096
NKC = 32
S_FULL = 8192
VIS_LEN = 2457
EPS_LN = 1e-5
EPS_ATTN = 1e-6

ENGS = ("sync", "tensor", "vector", "scalar", "gpsimd")


class Prog:

    def __init__(self):
        self.pend = {e: [] for e in ENGS}
        self.cnt = {}
        self.waited = {e: {} for e in ENGS}

    def tok(self, sem):
        return (sem, self.cnt.get(sem, 0))

    def op(self, eng, fn, waits=(), inc=None, amt=1):
        if inc is not None:
            self.cnt[inc] = self.cnt.get(inc, 0) + amt
        ws = [w for w in waits if w is not None and w[0] is not None and w[1] > 0]
        self.pend[eng].append((ws, fn, inc, amt))
        return (inc, self.cnt[inc]) if inc is not None else None

    def flush(self, block, sems):
        for eng in ENGS:
            ops = self.pend[eng]
            if not ops:
                continue
            self.pend[eng] = []
            waited = self.waited[eng]

            def body(b, ops=ops, waited=waited):
                for waits, fn, inc, amt in ops:
                    for s, v in waits:
                        if v > waited.get(s, 0):
                            b.wait_ge(sems[s], v)
                            waited[s] = v
                    r = fn(b)
                    if inc is not None:
                        if isinstance(r, list):
                            tot = 0
                            for inst, a in r:
                                inst.then_inc(sems[inc], a)
                                tot += a
                            assert tot == amt, (tot, amt)
                        else:
                            r.then_inc(sems[inc], amt)
            getattr(block, eng)(body)


def host_prep(inputs):
    X = np.ascontiguousarray(
        np.asarray(inputs["L0_attn_hidden_states"], np.float32).reshape(2 * S_FULL, HID))
    Wq = np.asarray(inputs["L0_attn_Wq"], np.float32)
    sel = np.concatenate([np.arange(h * 128, (h + 1) * 128) for h in range(0, 32, 4)])
    WqselT = np.ascontiguousarray(Wq[sel, :].T)
    wq_host = np.ascontiguousarray(
        WqselT.reshape(32, 128, 8, 128).transpose(2, 1, 0, 3).reshape(8, 128, 32 * 128))
    WoT = np.ascontiguousarray(np.asarray(inputs["L0_attn_Wo"], np.float32).T)
    w1 = np.asarray(inputs["L0_ffn_w1"], np.float32)
    w2 = np.asarray(inputs["L0_ffn_w2"], np.float32)
    w1_host = np.ascontiguousarray(
        w1.reshape(3, 32, 128, 32, 128).transpose(0, 3, 2, 1, 4).reshape(3, 32, 128, 32 * 128))
    w2_host = np.ascontiguousarray(
        w2.reshape(3, 32, 128, 32, 128).transpose(0, 3, 2, 1, 4).reshape(3, 32, 128, 32 * 128))
    b1c = np.ascontiguousarray(
        np.asarray(inputs["L0_ffn_b1"], np.float32).reshape(3, 32, 128).transpose(2, 0, 1)
        .reshape(128, 96))
    b2m = np.ascontiguousarray(np.asarray(inputs["L0_ffn_b2"], np.float32))
    gw = np.asarray(inputs["L0_ffn_gate_w"], np.float32)
    gwm = np.ascontiguousarray(gw[:HID].reshape(32, 128, 3).transpose(1, 0, 2).reshape(128, 96))
    gwl = np.ascontiguousarray(gw[HID].reshape(1, 3))
    gbr = np.ascontiguousarray(np.asarray(inputs["L0_ffn_gate_b"], np.float32).reshape(1, 3))
    bqr = np.ascontiguousarray(np.asarray(inputs["L0_attn_bq"], np.float32)[sel].reshape(1, 1024))
    Mh = np.ascontiguousarray(
        np.asarray(inputs["L0_attn_M"], np.float32).transpose(1, 0, 2).reshape(128, 8 * 128))
    zc = np.ascontiguousarray(np.asarray(inputs["L0_attn_z"], np.float32).T)
    ident = np.eye(128, dtype=np.float32)
    onesr = np.ones((1, TT), np.float32)

    shared = dict(wq=wq_host, wot=WoT, w1=w1_host, w2=w2_host, b1c=b1c, b2m=b2m,
                  gwm=gwm, gwl=gwl, gbr=gbr, bqr=bqr, mmat=Mh, zc=zc,
                  ident=ident, onesr=onesr)
    in_maps = []
    for c in range(NCORES):
        tok = np.arange(c * T_CORE, (c + 1) * T_CORE) % S_FULL
        maskv = (tok < VIS_LEN).astype(np.float32).reshape(1, T_CORE)
        meta = np.zeros((1, 8), np.float32)
        meta[0, 0] = 1.0 if c < 4 else 0.0
        meta[0, 1] = 0.0 if c < 4 else 1.0
        meta[0, 2] = 1.0 if c % 4 == 0 else 0.0
        m = dict(shared)
        m["x"] = np.ascontiguousarray(X[c * T_CORE:(c + 1) * T_CORE])
        m["maskv"] = maskv
        m["meta"] = meta
        in_maps.append(m)
    return in_maps


R_M0, R_RSTD0, R_NEGM0, R_STD0 = 0, 512, 1024, 1536
R_MA, R_RSTDA, R_NEGMA = 2048, 2560, 3072
R_RSTD2 = 3584
R_TMP, R_TMP2 = 4096, 4608
R_DEN = 5120
R_SABS = 5632
R_CELL = 6144
R_L = 0
R_E = 1536
R_S, R_RECS, R_A2, R_WTOP, R_W2ND = 3584, 4096, 4608, 5120, 5632
R_M3, R_RSTD3, R_NEGM3 = 5120, 5632, 4608
ROWS_LEN = 6208


BUILD_STAGE = "all"
DEBUG_TAPS = False


def build_nc(stage=None):
    stage = stage or BUILD_STAGE
    nc = bass.Bass("TRN2", target_bir_lowering=False)
    d_x = nc.dram_tensor("x", [T_CORE, HID], F32, kind="ExternalInput").ap()
    d_wq = nc.dram_tensor("wq", [8, 128, 32 * 128], F32, kind="ExternalInput").ap()
    d_wot = nc.dram_tensor("wot", [HID, HID], F32, kind="ExternalInput").ap()
    d_w1 = nc.dram_tensor("w1", [3, 32, 128, 32 * 128], F32, kind="ExternalInput").ap()
    d_w2 = nc.dram_tensor("w2", [3, 32, 128, 32 * 128], F32, kind="ExternalInput").ap()
    d_b1c = nc.dram_tensor("b1c", [128, 96], F32, kind="ExternalInput").ap()
    d_b2m = nc.dram_tensor("b2m", [3, HID], F32, kind="ExternalInput").ap()
    d_gwm = nc.dram_tensor("gwm", [128, 96], F32, kind="ExternalInput").ap()
    d_gwl = nc.dram_tensor("gwl", [1, 3], F32, kind="ExternalInput").ap()
    d_gbr = nc.dram_tensor("gbr", [1, 3], F32, kind="ExternalInput").ap()
    d_bqr = nc.dram_tensor("bqr", [1, 1024], F32, kind="ExternalInput").ap()
    d_m = nc.dram_tensor("mmat", [128, 8 * 128], F32, kind="ExternalInput").ap()
    d_zc = nc.dram_tensor("zc", [128, 8], F32, kind="ExternalInput").ap()
    d_ident = nc.dram_tensor("ident", [128, 128], F32, kind="ExternalInput").ap()
    d_onesr = nc.dram_tensor("onesr", [1, TT], F32, kind="ExternalInput").ap()
    d_maskv = nc.dram_tensor("maskv", [1, T_CORE], F32, kind="ExternalInput").ap()
    d_meta = nc.dram_tensor("meta", [1, 8], F32, kind="ExternalInput").ap()

    d_out = nc.dram_tensor("out", [T_CORE, HID], F32, kind="ExternalOutput").ap()

    d_wqbf = nc.dram_tensor("wqbf", [8, 128, 32 * 128], BF16, kind="Internal").ap()
    d_wobf = nc.dram_tensor("wobf", [32, 128, 8 * 128], BF16, kind="Internal").ap()
    d_w1bf = nc.dram_tensor("w1bf", [3, 32, 128, 32 * 128], BF16, kind="Internal").ap()
    d_w2bf = nc.dram_tensor("w2bf", [3, 32, 128, 32 * 128], BF16, kind="Internal").ap()
    d_x1s = nc.dram_tensor("x1s", [NT, 128, NKC * TT], F32, kind="Internal").ap()
    d_hs = nc.dram_tensor("hs", [NT, 128, NKC * TT], BF16, kind="Internal").ap()
    d_ccin = nc.dram_tensor("ccin", [1, 32], F32, kind="Internal").ap()
    d_ccout = nc.dram_tensor("ccout", [1, 32], F32, kind="Internal").ap()
    if DEBUG_TAPS:
        d_x1dbg = nc.dram_tensor("x1dbg", [NT, 128, NKC * TT], F32, kind="ExternalOutput").ap()
        d_hdbg = nc.dram_tensor("hdbg", [NT, 128, NKC * TT], F32, kind="ExternalOutput").ap()
        d_ardbg = nc.dram_tensor("ardbg", [1, 32], F32, kind="ExternalOutput").ap()
        d_rowdbg = nc.dram_tensor("rowdbg", [1, ROWS_LEN], F32, kind="ExternalOutput").ap()
        d_atdbg = nc.dram_tensor("atdbg", [128, NKC * TT], BF16, kind="ExternalOutput").ap()
        d_rtdbg = nc.dram_tensor("rtdbg", [128, 8 * TT], BF16, kind="ExternalOutput").ap()
        d_moedbg = nc.dram_tensor("moedbg", [128, NKC * TT], F32, kind="ExternalOutput").ap()
        d_rowdbg2 = nc.dram_tensor("rowdbg2", [1, ROWS_LEN], F32, kind="ExternalOutput").ap()
        d_cfdbg = nc.dram_tensor("cfdbg", [128, 3 * TT], BF16, kind="ExternalOutput").ap()

    P = Prog()
    SEMS = ["mm", "vv", "ss", "gg",
            "xin", "wq0", "wq1", "wo0", "wo1",
            "wa0", "wa1", "wb0", "wb1", "xc0", "xc1", "xc2", "xc3", "hld",
            "sx1", "sh", "sout", "spr", "smis", "lpr",
            "cv10", "cv11", "cv12", "cv20", "cv21", "cv22"]

    with ExitStack() as outer:
        sems = {s: outer.enter_context(nc.semaphore(s)) for s in SEMS}
        blk = outer.enter_context(nc.Block())
        sb = lambda n, shp, dt: outer.enter_context(nc.sbuf_tensor(n, shp, dt))

        bigbf = sb("bigbf", [128, NKC * TT], BF16)
        rows = sb("rows", [1, ROWS_LEN], F32)
        rows2 = sb("rows2", [3, TT], F32)
        bc0 = sb("bc0", [128, TT], F32)
        bc1 = sb("bc1", [128, TT], F32)
        tmpa = sb("tmpa", [128, 2 * TT], F32)
        sqpp = sb("sqpp", [128, 2 * TT], BF16)
        sqc2 = sb("sqc2", [128, 2 * TT], BF16)
        ident_s = sb("ident_s", [128, 128], F32)
        onesr_s = sb("onesr_s", [1, TT], F32)
        onesb_s = sb("onesb_s", [128, 1], BF16)
        gwm_s = sb("gwm_s", [128, 96], F32)
        gwb_s = sb("gwb_s", [128, 96], BF16)
        b1c_s = sb("b1c_s", [128, 96], F32)
        meta_s = sb("meta_s", [1, 8], F32)
        gwl_s = sb("gwl_s", [1, 3], F32)
        gbr_s = sb("gbr_s", [1, 3], F32)
        arst_s = sb("arst_s", [1, 32], F32)
        arout_s = sb("arout_s", [1, 32], F32)
        x1c_s = sb("x1c_s", [128, 2 * TT], F32)
        b2s_s = sb("b2s_s", [3, HID], BF16)
        colw = sb("colw", [128, 4], F32)

        def cell(i):
            return rows[0:1, R_CELL + i:R_CELL + i + 1]

        def row(off, n=TT):
            return rows[0:1, off:off + n]

        last = {}

        with ExitStack() as sA:
            sbA = lambda n, shp, dt: sA.enter_context(nc.sbuf_tensor(n, shp, dt))
            mz_s = sbA("mz_s", [128, 8 * 128 + 8], BF16)
            bqr_s = sbA("bqr_s", [1, 1024], F32)
            maskv_s = sbA("maskv_s", [1, T_CORE], F32)
            wsq_s = sbA("wsq_s", [1, 1024], F32)
            bc2 = sbA("bc2", [128, TT], F32)
            bc3 = sbA("bc3", [128, TT], F32)
            tmpb = sbA("tmpb", [128, 2 * TT], F32)

            def bcast(src_row, dst_tile, waits):
                tm = P.op("tensor",
                          lambda b, src_row=src_row: b.matmul(
                              bcH[:, :], onesr_s[0:1, 0:128], src_row,
                              start=True, stop=True),
                          waits=list(waits) + [last.get("bcfree"), P.tok("vv")],
                          inc="mm")
                te = P.op("scalar",
                          lambda b, dst_tile=dst_tile: b.activation(
                              dst_tile, bcH[:, :], AF.Copy),
                          waits=[tm], inc="ss")
                last["bcfree"] = te
                return te

            with ExitStack() as sP:
                wof_s = sP.enter_context(nc.sbuf_tensor("wof_s", [128, 4 * HID], F32))
                wqp_s = sP.enter_context(nc.sbuf_tensor("wqp_s", [128, 2 * HID], BF16))
                wsq_ps = sP.enter_context(nc.psum_tensor("wsq_ps", [1, 128], F32))

                def f(b):
                    r = []
                    for dst, src in ((ident_s[:, :], d_ident), (onesr_s[:, :], d_onesr),
                                     (bqr_s[:, :], d_bqr), (b1c_s[:, :], d_b1c),
                                     (gwm_s[:, :], d_gwm), (maskv_s[:, :], d_maskv),
                                     (meta_s[:, :], d_meta), (gwl_s[:, :], d_gwl),
                                     (gbr_s[:, :], d_gbr)):
                        r.append((b.dma_start(out=dst, in_=src), 16))
                    return r
                P.op("sync", f, inc="lpr", amt=16 * 9)
                t_lpr = P.tok("lpr")

                def f(b):
                    r = []
                    r.append((b.dma_start(out=mz_s[:, 0:8 * 128], in_=d_m), 16))
                    r.append((b.dma_start(out=mz_s[:, 8 * 128:8 * 128 + 8], in_=d_zc), 16))
                    r.append((b.dma_start(out=gwb_s[:, :], in_=d_gwm), 16))
                    r.append((b.dma_start(out=b2s_s[:, :], in_=d_b2m), 16))
                    return r
                if stage != "prep0":
                    P.op("gpsimd", f, inc="smis", amt=16 * 4)
                t_casts = P.tok("smis")
                t_onesb = P.op("vector", lambda b: b.memset(onesb_s[:, :], 1.0), inc="vv")

                for e in (range(3) if stage not in ("prep0", "prep1") else []):
                    def f1(b, e=e):
                        r = []
                        for fc in range(NKC):
                            r.append((b.dma_start(out=d_w1bf[e, fc], in_=d_w1[e, fc]), 16))
                        return r
                    P.op("gpsimd", f1, inc="cv1%d" % e, amt=16 * NKC)
                    def f2(b, e=e):
                        r = []
                        for dc in range(NKC):
                            r.append((b.dma_start(out=d_w2bf[e, dc], in_=d_w2[e, dc]), 16))
                        return r
                    P.op("gpsimd", f2, inc="cv2%d" % e, amt=16 * NKC)
                t_conv = [P.tok("cv1%d" % e) for e in range(3)] + \
                         [P.tok("cv2%d" % e) for e in range(3)]

                wq_wb = {}
                for fc in (range(8) if stage not in ("prep0", "prep1", "prep2") else []):
                    par = fc % 2
                    off = par * HID
                    tl = P.op("gpsimd",
                              lambda b, fc=fc, off=off: b.dma_start(
                                  out=wqp_s[:, off:off + HID], in_=d_wq[fc]),
                              waits=[last.get(("wqp", par))], inc="xc%d" % par, amt=16)
                    t_l = P.tok("xc%d" % par)
                    tw = P.op("gpsimd",
                              lambda b, fc=fc, off=off: b.dma_start(
                                  out=d_wqbf[fc], in_=wqp_s[:, off:off + HID]),
                              waits=[t_l], inc="spr", amt=16)
                    wq_wb[fc] = P.tok("spr")
                    if stage == "prep3a":
                        tm = P.op("tensor", lambda b: b.nop(),
                                  waits=[t_l, t_onesb, last.get("wsqev")], inc="mm")
                        tv = tm
                    else:
                        def fm(b, off=off):
                            for kc in range(NKC):
                                i = b.matmul(wsq_ps[0:1, :], onesb_s[:, 0:1],
                                             wqp_s[:, off + kc * 128: off + (kc + 1) * 128],
                                             start=(kc == 0), stop=(kc == NKC - 1))
                            return i
                        tm = P.op("tensor", fm,
                                  waits=[t_l, t_onesb, last.get("wsqev")], inc="mm")
                        tv = P.op("vector",
                                  lambda b, fc=fc: b.tensor_copy(
                                      wsq_s[0:1, fc * 128:(fc + 1) * 128], wsq_ps[0:1, :]),
                                  waits=[tm], inc="vv")
                    last["wsqev"] = tv
                    last[("wqp", par)] = tm
                t_wsq = P.tok("vv")

                for hc in (range(8) if stage not in ("prep0", "prep1", "prep2", "prep3", "prep3a", "prep3b") else []):
                    par = hc % 2
                    off = par * HID
                    def fload(b, hc=hc):
                        r = []
                        for g in range(4):
                            ig = b.dma_start(
                                out=wof_s[:, g * HID:(g + 1) * HID],
                                in_=d_wot[(4 * hc + g) * 128:(4 * hc + g + 1) * 128, :])
                            r.append((ig, 16))
                        return r
                    tl = P.op("sync", fload, waits=[last.get("wofcast")],
                              inc="lpr", amt=64)
                    t_l = P.tok("lpr")
                    if stage == "prep4a":
                        last["wofadd"] = t_l
                        continue
                    def fadd(b):
                        b.tensor_add(wof_s[:, 0:HID], wof_s[:, 0:HID],
                                     wof_s[:, HID:2 * HID])
                        b.tensor_add(wof_s[:, 2 * HID:3 * HID], wof_s[:, 2 * HID:3 * HID],
                                     wof_s[:, 3 * HID:4 * HID])
                        return b.tensor_add(wof_s[:, 0:HID], wof_s[:, 0:HID],
                                            wof_s[:, 2 * HID:3 * HID])
                    tadd = P.op("vector", fadd, waits=[t_l], inc="vv")
                    last["wofadd"] = tadd
                    tc = P.op("scalar",
                              lambda b, off=off: b.activation(
                                  wqp_s[:, off:off + HID], wof_s[:, 0:HID],
                                  AF.Copy),
                              waits=[tadd, last.get(("wqps", par)),
                                     wq_wb.get(6 + par), last.get(("wqp", par))],
                              inc="ss")
                    last["wofcast"] = tc
                    if stage == "prep4b":
                        continue
                    def fstore(b, hc=hc, off=off):
                        src = wqp_s[:, off:off + HID].rearrange("p (oc o) -> p oc o", oc=32)
                        dst = d_wobf[:, :, hc * 128:(hc + 1) * 128].transpose([1, 0, 2])
                        return b.dma_start(out=dst, in_=src)
                    ts = P.op("gpsimd", fstore, waits=[tc], inc="spr", amt=16)
                    last[("wqps", par)] = P.tok("spr")
                t_prep = [P.tok("spr"), P.tok("smis"), t_lpr, t_wsq, P.tok("ss")]
                P.flush(blk, sems)

            NT_EFF = 0 if stage.startswith("prep") else (1 if stage == "a1" else NT)
            xt_s = sbA("xt_s", [128, HID], F32)
            xbf = sbA("xbf", [128, NKC * TT], BF16)
            sig_s = sbA("sig_s", [128, 8 * TT], BF16)
            hpp_s = sbA("hpp_s", [128, 4 * TT], BF16)
            wq_s = sbA("wq_s", [128, 4 * 2048], BF16)
            wo_s = sbA("wo_s", [128, 2 * 2048], BF16)
            psA = lambda n, shp: sA.enter_context(nc.psum_tensor(n, shp, F32))
            tp0 = psA("tp0", [128, TT])
            tp1 = psA("tp1", [128, TT])
            mm0 = psA("mm0", [128, TT])
            mm1 = psA("mm1", [128, TT])
            statE = psA("statE", [1, TT])
            statF = psA("statF", [1, TT])
            tok0G = psA("tok0G", [1, TT])
            bcH = psA("bcH", [128, TT])
            t_l0 = None
            for t in range(NT_EFF):
                ev_tok = {}
                for tc in range(4):
                    wl = [last.get("xtfree")]
                    if t == 0 and tc == 0:
                        wl += t_prep
                    tl = P.op("sync",
                              lambda b, t=t, tc=tc: b.dma_start(
                                  out=xt_s[:, :],
                                  in_=d_x[t * TT + tc * 128: t * TT + (tc + 1) * 128, :]),
                              waits=wl, inc="xin", amt=16)
                    t_l = P.tok("xin")
                    for kg in range(8):
                        pp = tp0 if kg % 2 == 0 else tp1
                        def ftp(b, pp=pp, kg=kg):
                            for j in range(4):
                                i = b.transpose(pp[:, j * 128:(j + 1) * 128],
                                                xt_s[:, (kg * 4 + j) * 128:(kg * 4 + j + 1) * 128],
                                                ident_s[:, :])
                            return i
                        tmm = P.op("tensor", ftp,
                                   waits=[t_l, last.get(("tp", kg % 2)),
                                          (t_lpr if t == 0 and tc == 0 else None)] +
                                         [last.get(("xbfrd", kg * 4 + j)) for j in range(4)],
                                   inc="mm")
                        eng = "vector" if kg % 2 == 0 else "scalar"
                        dst = xbf[:, :].rearrange("p (kc v) -> p kc v", kc=NKC)[
                            :, kg * 4:(kg + 1) * 4, tc * 128:(tc + 1) * 128]
                        emit = (lambda b, pp=pp, dst=dst: b.tensor_copy(
                                    dst, pp[:, :].rearrange("p (j v) -> p j v", j=4))) \
                               if eng == "vector" else \
                               (lambda b, pp=pp, dst=dst: b.activation(
                                    dst, pp[:, :].rearrange("p (j v) -> p j v", j=4), AF.Copy))
                        tev = P.op(eng, emit, waits=[tmm],
                                   inc="vv" if eng == "vector" else "ss")
                        last[("tp", kg % 2)] = tev
                        for j in range(4):
                            ev_tok[(kg * 4 + j, tc)] = tev
                    last["xtfree"] = P.tok("mm")
                smm_last = None
                for kc in range(NKC):
                    par = kc % 2
                    wsq = [ev_tok[(kc, tc)] for tc in range(4)] + [last.get(("sq", par))]
                    ts1 = P.op("scalar",
                               lambda b, kc=kc, par=par: b.activation(
                                   sqpp[:, par * TT:(par + 1) * TT],
                                   xbf[:, kc * TT:(kc + 1) * TT], AF.Square),
                               waits=wsq, inc="ss")
                    wmm = [ts1, t_onesb, (last.get("rowsA_rd") if kc == 0 else None)]
                    tm1 = P.op("tensor",
                               lambda b, kc=kc: b.matmul(
                                   statE[0:1, :], onesb_s[:, 0:1],
                                   xbf[:, kc * TT:(kc + 1) * TT],
                                   start=(kc == 0), stop=(kc == NKC - 1)),
                               waits=wmm, inc="mm")
                    tm2 = P.op("tensor",
                               lambda b, par=par, kc=kc: b.matmul(
                                   statF[0:1, :], onesb_s[:, 0:1],
                                   sqpp[:, par * TT:(par + 1) * TT],
                                   start=(kc == 0), stop=(kc == NKC - 1)),
                               waits=[], inc="mm")
                    last[("sq", par)] = tm2
                    smm_last = tm2

                def f(b):
                    b.tensor_scalar_mul(row(R_M0), statE[0:1, :], 1.0 / HID)
                    b.tensor_scalar_mul(row(R_TMP), statF[0:1, :], 1.0 / HID)
                    b.tensor_tensor(row(R_TMP2), row(R_M0), row(R_M0), ALU.mult)
                    b.tensor_sub(row(R_TMP), row(R_TMP), row(R_TMP2))
                    b.tensor_scalar_add(row(R_TMP), row(R_TMP), EPS_LN)
                    b.reciprocal(row(R_TMP), row(R_TMP))
                    return b.tensor_scalar_mul(row(R_NEGM0), row(R_M0), -1.0)
                t_r0a = P.op("vector", f, waits=[smm_last], inc="vv")
                t_r0b = P.op("scalar",
                             lambda b: b.activation(row(R_RSTD0), row(R_TMP), AF.Sqrt),
                             waits=[t_r0a], inc="ss")
                t_r0c = P.op("vector",
                             lambda b: b.reciprocal(row(R_STD0), row(R_RSTD0)),
                             waits=[t_r0b], inc="vv")
                last["rowsA_rd"] = t_r0a
                t_bc_rstd0 = bcast(row(R_RSTD0), bc1[:, :], [t_r0b])

                sig_tok = {}
                WQ_SEMS = ("wq0", "wq1", "xc2", "xc3")
                if ("wqu",) not in last:
                    last[("wqu",)] = 0
                for fc in range(8):
                    toks = []
                    for half in range(2):
                        u = last[("wqu",)]
                        last[("wqu",)] = u + 1
                        s = u % 4
                        sem = WQ_SEMS[s]
                        P.op("sync",
                             lambda b, fc=fc, half=half, s=s: b.dma_start(
                                 out=wq_s[:, s * 2048:(s + 1) * 2048],
                                 in_=d_wqbf[fc][:, half * 2048:(half + 1) * 2048]),
                             waits=[last.get(("wqslot", s)),
                                    (wq_wb[fc] if t == 0 else None)],
                             inc=sem, amt=16)
                        toks.append((P.tok(sem), s))
                    pq = mm0 if fc % 2 == 0 else mm1
                    (tl0, s0), (tl1, s1) = toks
                    wmm = [tl0, tl1, t_r0a, t_r0c,
                           last.get(("mmq", fc % 2))]
                    wmm += [ev_tok[(kc, tc)] for kc in range(NKC) for tc in range(4)] \
                        if fc == 0 else []
                    def fq(b, fc=fc, pq=pq, s0=s0, s1=s1):
                        for half, ss in ((0, s0), (1, s1)):
                            for k in range(16):
                                kc = half * 16 + k
                                b.matmul(pq[:, :],
                                         wq_s[:, ss * 2048 + k * 128: ss * 2048 + (k + 1) * 128],
                                         xbf[:, kc * TT:(kc + 1) * TT],
                                         start=(kc == 0), stop=False)
                        b.matmul(pq[:, :], wsq_s[0:1, fc * 128:(fc + 1) * 128],
                                 row(R_NEGM0), start=False, stop=False)
                        return b.matmul(pq[:, :], bqr_s[0:1, fc * 128:(fc + 1) * 128],
                                        row(R_STD0), start=False, stop=True)
                    tq = P.op("tensor", fq, waits=wmm, inc="mm")
                    last[("wqslot", s0)] = tq
                    last[("wqslot", s1)] = tq
                    par = fc % 2
                    t1 = P.op("vector",
                              lambda b, pq=pq, par=par: b.tensor_mul(
                                  tmpa[:, par * TT:(par + 1) * TT], pq[:, :], bc1[:, :]),
                              waits=[tq, t_bc_rstd0, last.get(("tmpaA", par))], inc="vv")
                    t2 = P.op("vector",
                              lambda b, par=par: b.tensor_scalar_min(
                                  tmpb[:, par * TT:(par + 1) * TT],
                                  tmpa[:, par * TT:(par + 1) * TT], 0.0),
                              waits=[t1], inc="vv")
                    t3 = P.op("scalar",
                              lambda b, par=par: b.activation(
                                  tmpb[:, par * TT:(par + 1) * TT],
                                  tmpb[:, par * TT:(par + 1) * TT], AF.Exp),
                              waits=[t2], inc="ss")
                    t4 = P.op("vector",
                              lambda b, fc=fc, par=par: b.scalar_tensor_tensor(
                                  sig_s[:, fc * TT:(fc + 1) * TT],
                                  tmpa[:, par * TT:(par + 1) * TT], 0.0,
                                  tmpb[:, par * TT:(par + 1) * TT], ALU.max, ALU.add),
                              waits=[t3], inc="vv")
                    last[("mmq", fc % 2)] = t4
                    last[("tmpaA", par)] = t4
                    sig_tok[fc] = t4

                retr_tok = {}
                for h in range(8):
                    pq = mm0 if h % 2 == 0 else mm1
                    tnum = P.op("tensor",
                                lambda b, h=h, pq=pq: b.matmul(
                                    pq[:, :], mz_s[:, h * 128:(h + 1) * 128],
                                    sig_s[:, h * TT:(h + 1) * TT],
                                    start=True, stop=True),
                                waits=[sig_tok[h], t_casts, last.get(("mmq", h % 2))],
                                inc="mm")
                    tden = P.op("tensor",
                                lambda b, h=h: b.matmul(
                                    statF[0:1, :], mz_s[:, 8 * 128 + h:8 * 128 + h + 1],
                                    sig_s[:, h * TT:(h + 1) * TT],
                                    start=True, stop=True),
                                waits=[last.get("denF"), (t_r0a if h == 0 else None)],
                                inc="mm")
                    td1 = P.op("vector",
                               lambda b: b.tensor_scalar_add(
                                   row(R_DEN), statF[0:1, :], EPS_ATTN),
                               waits=[tden, last.get("denrow")], inc="vv")
                    last["denF"] = td1
                    td2 = P.op("vector",
                               lambda b: b.reciprocal(row(R_DEN), row(R_DEN)),
                               waits=[td1], inc="vv")
                    tb = bcast(row(R_DEN), bc2[:, :], [td2])
                    last["denrow"] = tb
                    tev = P.op("vector",
                               lambda b, h=h, pq=pq: b.tensor_mul(
                                   sig_s[:, h * TT:(h + 1) * TT], pq[:, :], bc2[:, :]),
                               waits=[tnum, tb], inc="vv")
                    last[("mmq", h % 2)] = tev
                    retr_tok[h] = tev
                retr_all = [retr_tok[h] for h in range(8)]

                at_tok = {}
                amm_last = None
                for og in range(16):
                    par = og % 2
                    sem = "wo%d" % par
                    tl = P.op("sync",
                              lambda b, og=og, par=par: b.dma_start(
                                  out=wo_s[:, par * 2048:(par + 1) * 2048]
                                  .rearrange("p (o v) -> p o v", o=2),
                                  in_=d_wobf.transpose([1, 0, 2])[:, og * 2:og * 2 + 2, :]),
                              waits=[last.get(("wos", par)),
                                     (t_prep[0] if t == 0 and og < 2 else None)],
                              inc=sem, amt=16)
                    t_l = P.tok(sem)
                    for j in range(2):
                        oc = og * 2 + j
                        pa = mm0 if oc % 2 == 0 else mm1
                        wmm = [t_l, last.get(("mmq", oc % 2))] + \
                              (retr_all if oc == 0 else [])
                        def fa(b, pa=pa, par=par, j=j):
                            for hc in range(8):
                                i = b.matmul(pa[:, :],
                                             wo_s[:, par * 2048 + j * 1024 + hc * 128:
                                                  par * 2048 + j * 1024 + (hc + 1) * 128],
                                             sig_s[:, hc * TT:(hc + 1) * TT],
                                             start=(hc == 0), stop=(hc == 7))
                            return i
                        ta = P.op("tensor", fa, waits=wmm, inc="mm")
                        last[("wos", par)] = ta
                        te1 = P.op("scalar",
                                   lambda b, pa=pa, oc=oc: b.activation(
                                       bigbf[:, oc * TT:(oc + 1) * TT], pa[:, :],
                                       AF.Copy),
                                   waits=[ta, last.get(("bigw", oc))], inc="ss")
                        spar = oc % 2
                        te2 = P.op("scalar",
                                   lambda b, pa=pa, spar=spar: b.activation(
                                       sqpp[:, spar * TT:(spar + 1) * TT], pa[:, :],
                                       AF.Square),
                                   waits=[ta, last.get(("sq", spar))], inc="ss")
                        last[("mmq", oc % 2)] = te2
                        wm1 = [te1, te2] + ([t_r0a, last.get("rowsA_rd")] if oc == 0 else [])
                        tm1 = P.op("tensor",
                                   lambda b, oc=oc: b.matmul(
                                       statE[0:1, :], onesb_s[:, 0:1],
                                       bigbf[:, oc * TT:(oc + 1) * TT],
                                       start=(oc == 0), stop=(oc == NKC - 1)),
                                   waits=wm1, inc="mm")
                        tm2 = P.op("tensor",
                                   lambda b, spar=spar, oc=oc: b.matmul(
                                       statF[0:1, :], onesb_s[:, 0:1],
                                       sqpp[:, spar * TT:(spar + 1) * TT],
                                       start=(oc == 0), stop=(oc == NKC - 1)),
                                   waits=([last.get("denF")] if oc == 0 else []),
                                   inc="mm")
                        last[("sq", spar)] = tm2
                        amm_last = tm2
                        at_tok[oc] = te1

                def f(b):
                    b.tensor_scalar_mul(row(R_MA), statE[0:1, :], 1.0 / HID)
                    b.tensor_scalar_mul(row(R_TMP), statF[0:1, :], 1.0 / HID)
                    b.tensor_tensor(row(R_TMP2), row(R_MA), row(R_MA), ALU.mult)
                    b.tensor_sub(row(R_TMP), row(R_TMP), row(R_TMP2))
                    b.tensor_scalar_add(row(R_TMP), row(R_TMP), EPS_LN)
                    b.reciprocal(row(R_TMP), row(R_TMP))
                    return b.tensor_scalar_mul(row(R_NEGMA), row(R_MA), -1.0)
                t_r1a = P.op("vector", f, waits=[amm_last], inc="vv")
                t_r1b = P.op("scalar",
                             lambda b: b.activation(row(R_RSTDA), row(R_TMP), AF.Sqrt),
                             waits=[t_r1a], inc="ss")
                last["rowsA_rd"] = t_r1a
                t_bc_negm1 = bcast(row(R_NEGMA), bc2[:, :], [t_r1a, retr_all[-1]])
                t_bc_rstd1 = bcast(row(R_RSTDA), bc3[:, :], [t_r1b])

                x1st_tok = {}
                x1sq_last = None
                for oc in range(NKC):
                    par = oc % 2
                    slot = oc % 2
                    wts = [t_bc_negm1, t_bc_rstd1, at_tok[oc],
                           last.get(("tmpaA", par)), last.get(("x1slot", slot))]
                    t1 = P.op("vector",
                              lambda b, oc=oc, par=par: b.tensor_add(
                                  tmpa[:, par * TT:(par + 1) * TT],
                                  bigbf[:, oc * TT:(oc + 1) * TT], bc2[:, :]),
                              waits=wts, inc="vv")
                    t2 = P.op("vector",
                              lambda b, par=par: b.tensor_mul(
                                  tmpa[:, par * TT:(par + 1) * TT],
                                  tmpa[:, par * TT:(par + 1) * TT], bc3[:, :]),
                              waits=[t1], inc="vv")
                    t3 = P.op("vector",
                              lambda b, oc=oc, par=par, slot=slot: b.tensor_add(
                                  x1c_s[:, slot * TT:(slot + 1) * TT],
                                  tmpa[:, par * TT:(par + 1) * TT],
                                  xbf[:, oc * TT:(oc + 1) * TT]),
                              waits=[t2], inc="vv")
                    last[("tmpaA", par)] = t3
                    last[("bigw", oc)] = t3
                    last[("xbfrd", oc)] = t3
                    ts1 = P.op("scalar",
                               lambda b, par=par, slot=slot: b.activation(
                                   sqpp[:, par * TT:(par + 1) * TT],
                                   x1c_s[:, slot * TT:(slot + 1) * TT], AF.Square),
                               waits=[t3, last.get(("sq", par))], inc="ss")
                    tm = P.op("tensor",
                              lambda b, oc=oc, par=par: b.matmul(
                                  statE[0:1, :], onesb_s[:, 0:1],
                                  sqpp[:, par * TT:(par + 1) * TT],
                                  start=(oc == 0), stop=(oc == NKC - 1)),
                              waits=[ts1] + ([t_r1a, last.get("rowsA_rd")] if oc == 0 else []),
                              inc="mm")
                    last[("sq", par)] = tm
                    x1sq_last = tm
                    if slot == 1:
                        g = oc // 2
                        tg = P.op("scalar",
                                  lambda b, t=t, g=g: b.dma_start(
                                      out=d_x1s[t][:, g * 2 * TT:(g + 1) * 2 * TT],
                                      in_=x1c_s[:, :]),
                                  waits=[last[("tmpaA", 0)], last[("tmpaA", 1)]],
                                  inc="sx1", amt=16)
                        t_g = P.tok("sx1")
                        for s_ in range(2):
                            last[("x1slot", s_)] = t_g
                        x1st_tok[g] = t_g

                def f(b):
                    b.tensor_scalar_mul(row(R_TMP), statE[0:1, :], 1.0 / HID)
                    b.tensor_tensor(row(R_TMP2), row(R_M0), row(R_M0), ALU.mult)
                    b.tensor_sub(row(R_TMP), row(R_TMP), row(R_TMP2))
                    b.tensor_scalar_add(row(R_TMP), row(R_TMP), EPS_LN)
                    return b.reciprocal(row(R_TMP), row(R_TMP))
                t_r2a = P.op("vector", f, waits=[x1sq_last], inc="vv")
                t_r2b = P.op("scalar",
                             lambda b: b.activation(row(R_RSTD2), row(R_TMP), AF.Sqrt),
                             waits=[t_r2a], inc="ss")
                last["rowsA_rd"] = t_r2a
                t_bc_negm0 = bcast(row(R_NEGM0), bc0[:, :], [t_r2a])
                t_bc_rstd2 = bcast(row(R_RSTD2), bc3[:, :],
                                   [t_r2b, last[("tmpaA", 0)], last[("tmpaA", 1)]])

                if t == 0:
                    def f(b):
                        b.tensor_copy(cell(19), rows[0:1, R_NEGM0:R_NEGM0 + 1])
                        return b.tensor_copy(cell(20), rows[0:1, R_RSTD2:R_RSTD2 + 1])
                    t_c0 = P.op("vector", f, waits=[t_r2a, t_r2b, t_bc_rstd2], inc="vv")
                    tmc = P.op("tensor",
                               lambda b: b.matmul(
                                   bcH[:, 0:2], onesr_s[0:1, 0:128],
                                   rows[0:1, R_CELL + 19:R_CELL + 21],
                                   start=True, stop=True),
                               waits=[t_c0, last.get("bcfree")], inc="mm")
                    t_cols = P.op("scalar",
                                  lambda b: b.activation(colw[:, 0:2], bcH[:, 0:2], AF.Copy),
                                  waits=[tmc], inc="ss")
                    last["bcfree"] = t_cols

                habs_last = None
                tg_prev = None
                for oc in range(NKC):
                    par = oc % 2
                    slot = oc % 4
                    sem = "xc%d" % par
                    tl = P.op("scalar",
                              lambda b, t=t, oc=oc, par=par: b.dma_start(
                                  out=tmpb[:, par * TT:(par + 1) * TT],
                                  in_=d_x1s[t][:, oc * TT:(oc + 1) * TT]),
                              waits=[last.get(("tmpbA", par))] +
                                    ([P.tok("sx1")] if oc == 0 else []),
                              inc=sem, amt=16)
                    t_l = P.tok(sem)
                    t1 = P.op("vector",
                              lambda b, par=par: b.tensor_add(
                                  tmpa[:, par * TT:(par + 1) * TT],
                                  tmpb[:, par * TT:(par + 1) * TT], bc0[:, :]),
                              waits=[t_l, t_bc_negm0, last.get(("tmpaA", par))],
                              inc="vv")
                    t2 = P.op("vector",
                              lambda b, par=par, slot=slot: b.tensor_mul(
                                  hpp_s[:, slot * TT:(slot + 1) * TT],
                                  tmpa[:, par * TT:(par + 1) * TT], bc3[:, :]),
                              waits=[t1, t_bc_rstd2, last.get(("hslot", slot))],
                              inc="vv")
                    last[("tmpaA", par)] = t2
                    ts1 = P.op("scalar",
                               lambda b, par=par, slot=slot: b.activation(
                                   sqpp[:, par * TT:(par + 1) * TT],
                                   hpp_s[:, slot * TT:(slot + 1) * TT], AF.Abs),
                               waits=[t2, last.get(("sq", par))], inc="ss")
                    tm = P.op("tensor",
                              lambda b, par=par, oc=oc: b.matmul(
                                  statF[0:1, :], onesb_s[:, 0:1],
                                  sqpp[:, par * TT:(par + 1) * TT],
                                  start=(oc == 0), stop=(oc == NKC - 1)),
                              waits=[ts1] + ([t_r1a, t_r2a] if oc == 0 else []),
                              inc="mm")
                    last[("sq", par)] = tm
                    habs_last = tm
                    if t == 0:
                        tc1 = P.op("vector",
                                   lambda b, par=par: b.tensor_add(
                                       colw[:, 2:3], tmpb[:, par * TT:par * TT + 1],
                                       colw[:, 0:1]),
                                   waits=[t_l, t_cols, (tg_prev or None)], inc="vv")
                        tc2 = P.op("vector",
                                   lambda b: b.tensor_mul(colw[:, 3:4], colw[:, 2:3],
                                                          colw[:, 1:2]),
                                   waits=[tc1], inc="vv")
                        tg_prev = P.op("tensor",
                                       lambda b, oc=oc: b.matmul(
                                           tok0G[0:1, 0:3], colw[:, 3:4],
                                           gwm_s[:, oc * 3:(oc + 1) * 3],
                                           start=(oc == 0), stop=(oc == NKC - 1)),
                                       waits=[tc2, t_lpr],
                                       inc="mm")
                    last[("tmpbA", par)] = (tm if t != 0 else P.tok("vv"))
                    if slot == 3:
                        g = oc // 4
                        tg = P.op("scalar",
                                  lambda b, t=t, g=g: b.dma_start(
                                      out=d_hs[t][:, g * 4 * TT:(g + 1) * 4 * TT],
                                      in_=hpp_s[:, :]),
                                  waits=[last[("tmpaA", 0)], last[("tmpaA", 1)]],
                                  inc="sh", amt=16)
                        t_g = P.tok("sh")
                        for s_ in range(4):
                            last[("hslot", s_)] = t_g
                if t == 0:
                    t_l0 = P.op("vector",
                                lambda b: b.tensor_copy(
                                    rows[0:1, R_CELL + 2:R_CELL + 5], tok0G[0:1, 0:3]),
                                waits=[tg_prev], inc="vv")

                def f(b, t=t):
                    b.tensor_copy(row(R_SABS), statF[0:1, :])
                    b.tensor_mul(row(R_TMP), row(R_SABS),
                                 maskv_s[0:1, t * TT:(t + 1) * TT])
                    b.tensor_reduce(cell(15), row(R_TMP), AX.X, ALU.add)
                    return b.tensor_reduce(cell(16), row(R_SABS), AX.X, ALU.add)
                t_sega = P.op("vector", f, waits=[habs_last], inc="vv")
                def f(b, t=t):
                    if t == 0:
                        b.tensor_copy(cell(0), cell(15))
                        return b.tensor_copy(cell(1), cell(16))
                    b.tensor_add(cell(25), cell(0), cell(15))
                    return b.tensor_add(cell(26), cell(1), cell(16))
                t_segb = P.op("vector", f, waits=[t_sega], inc="vv")
                if t == 0:
                    t_seg = t_segb
                else:
                    def f(b):
                        b.tensor_copy(cell(0), cell(25))
                        return b.tensor_copy(cell(1), cell(26))
                    t_seg = P.op("vector", f, waits=[t_segb], inc="vv")
                last["rowsA_rd"] = t_sega

            do_ar = stage in ("ar", "all") and NT_EFF == NT
            def f1(b):
                return b.tensor_sub(cell(1), cell(1), cell(0))
            def f2(b):
                b.tensor_scalar(arst_s[0:1, 0:1], cell(0), meta_s[0:1, 0:1], None, ALU.mult)
                b.tensor_scalar(arst_s[0:1, 1:2], cell(1), meta_s[0:1, 0:1], None, ALU.mult)
                b.tensor_scalar(arst_s[0:1, 2:3], cell(0), meta_s[0:1, 1:2], None, ALU.mult)
                b.tensor_scalar(arst_s[0:1, 3:4], cell(1), meta_s[0:1, 1:2], None, ALU.mult)
                return b.tensor_scalar(rows[0:1, R_CELL + 2:R_CELL + 5],
                                       rows[0:1, R_CELL + 2:R_CELL + 5],
                                       meta_s[0:1, 2:3], None, ALU.mult)
            def f3(b):
                b.tensor_scalar(arst_s[0:1, 4:7], rows[0:1, R_CELL + 2:R_CELL + 5],
                                meta_s[0:1, 0:1], None, ALU.mult)
                b.tensor_scalar(arst_s[0:1, 7:10], rows[0:1, R_CELL + 2:R_CELL + 5],
                                meta_s[0:1, 1:2], None, ALU.mult)
                return b.memset(arst_s[0:1, 10:32], 0.0)
            if do_ar:
                ta1 = P.op("vector", f1, waits=[t_seg, t_l0, t_lpr], inc="vv")
                ta2 = P.op("vector", f2, waits=[ta1], inc="vv")
                t_arst = P.op("vector", f3, waits=[ta2], inc="vv")
                P.op("gpsimd", lambda b: b.dma_start(out=d_ccin, in_=arst_s[0:1, :]),
                     waits=[t_arst], inc="smis", amt=16)
                t_cc = P.op("gpsimd",
                            lambda b: b.collective_compute(
                                "AllReduce", ALU.add,
                                replica_groups=[[0, 1, 2, 3, 4, 5, 6, 7]],
                                ins=[d_ccin], outs=[d_ccout]),
                            waits=[P.tok("smis")], inc="gg")
                P.op("gpsimd", lambda b: b.dma_start(out=arout_s[0:1, :], in_=d_ccout),
                     waits=[t_cc], inc="smis", amt=16)
            t_ar = P.tok("smis")

            if DEBUG_TAPS:
                def ftap2(b):
                    r = []
                    r.append((b.dma_start(out=d_rowdbg, in_=rows[0:1, :]), 16))
                    r.append((b.dma_start(out=d_atdbg, in_=bigbf[:, :]), 16))
                    r.append((b.dma_start(out=d_rtdbg, in_=sig_s[:, :]), 16))
                    return r
                P.op("gpsimd", ftap2,
                     waits=[P.tok("vv"), P.tok("ss"), P.tok("mm")], inc="smis", amt=48)
                def ftap(b):
                    r = []
                    for t_ in range(NT):
                        r.append((b.dma_start(out=d_x1dbg[t_], in_=d_x1s[t_]), 16))
                        r.append((b.dma_start(out=d_hdbg[t_], in_=d_hs[t_]), 16))
                    return r
                P.op("gpsimd", ftap,
                     waits=[P.tok("sx1"), P.tok("sh")], inc="smis", amt=16 * 2 * NT)
                P.op("gpsimd", lambda b: b.dma_start(out=d_ardbg, in_=arout_s[0:1, :]),
                     waits=[t_ar], inc="smis", amt=16)
            joinA = [P.tok(s) for s in ("mm", "vv", "ss", "gg", "sx1", "sh",
                                        "smis", "spr", "xin", "wq0", "wq1",
                                        "wo0", "wo1", "xc0", "xc1")]
            P.flush(blk, sems)

        if stage != "all":
            P.flush(blk, sems)
            return nc
        with ExitStack() as sC:
            sbC = lambda n, shp, dt: sC.enter_context(nc.sbuf_tensor(n, shp, dt))
            act_s = sbC("act_s", [128, NKC * TT], BF16)
            moeacc = sbC("moeacc", [128, NKC * TT], F32)
            ws_s = sbC("ws_s", [128, 4 * 2048], BF16)
            x2t_s = sbC("x2t_s", [128, TT], F32)
            coefb = sbC("coefb", [128, 3 * TT], BF16)
            coef3 = sbC("coef3", [3, TT], BF16)

            psC = lambda n, shp: sC.enter_context(nc.psum_tensor(n, shp, F32))
            pre0 = psC("pre0", [128, TT])
            pre1 = psC("pre1", [128, TT])
            out0 = psC("out0", [128, TT])
            out1 = psC("out1", [128, TT])
            rowsC = psC("rowsC", [3, TT])
            statF2 = psC("statF2", [1, TT])
            bcH2 = psC("bcH2", [128, TT])
            tpo0 = psC("tpo0", [128, TT])
            tpo1 = bcH2

            last = {"bcfree": None}

            def bcast2(src_row, dst_tile, waits, dst2=None):
                tm = P.op("tensor",
                          lambda b, src_row=src_row: b.matmul(
                              bcH2[:, :], onesr_s[0:1, 0:128], src_row,
                              start=True, stop=True),
                          waits=list(waits) + [last.get("bcfree"),
                                               last.get(("tpoev", 1))],
                          inc="mm")
                te = P.op("scalar",
                          lambda b, dst_tile=dst_tile: b.activation(
                              dst_tile, bcH2[:, :], AF.Copy),
                          waits=[tm], inc="ss")
                last["bcfree"] = te
                return te

            def g1(b):
                b.tensor_scalar_mul(cell(17), arout_s[0:1, 0:1], 1.0 / (VIS_LEN * HID))
                b.tensor_scalar_mul(cell(18), arout_s[0:1, 1:2],
                                    1.0 / ((S_FULL - VIS_LEN) * HID))
                b.tensor_scalar(cell(15), arout_s[0:1, 2:3], meta_s[0:1, 1:2], None, ALU.mult)
                return b.tensor_scalar(cell(16), arout_s[0:1, 3:4], meta_s[0:1, 1:2],
                                       None, ALU.mult)
            tg1 = P.op("vector", g1, waits=joinA + [t_ar], inc="vv")
            def g2(b):
                b.tensor_scalar(cell(15), cell(15), 1.0 / (VIS_LEN * HID), None, ALU.mult)
                b.tensor_scalar(cell(16), cell(16), 1.0 / ((S_FULL - VIS_LEN) * HID),
                                None, ALU.mult)
                b.tensor_scalar(cell(19), cell(17), meta_s[0:1, 0:1], None, ALU.mult)
                return b.tensor_scalar(cell(20), cell(18), meta_s[0:1, 0:1], None, ALU.mult)
            tg2 = P.op("vector", g2, waits=[tg1], inc="vv")
            def g3(b):
                b.tensor_add(cell(17), cell(19), cell(15))
                return b.tensor_add(cell(18), cell(20), cell(16))
            tg3 = P.op("vector", g3, waits=[tg2], inc="vv")
            tg4 = P.op("vector", lambda b: b.tensor_scalar_add(cell(21), cell(17), 1e-8),
                       waits=[tg3], inc="vv")
            tg5 = P.op("vector", lambda b: b.reciprocal(cell(22), cell(21)),
                       waits=[tg4], inc="vv")
            tg6 = P.op("vector", lambda b: b.tensor_mul(cell(5), cell(18), cell(22)),
                       waits=[tg5], inc="vv")
            def g7(b):
                b.tensor_scalar(rows[0:1, R_CELL + 2:R_CELL + 5], arout_s[0:1, 4:7],
                                meta_s[0:1, 0:1], None, ALU.mult)
                b.tensor_scalar(rows[0:1, R_CELL + 28:R_CELL + 31], arout_s[0:1, 7:10],
                                meta_s[0:1, 1:2], None, ALU.mult)
                return b.tensor_scalar(rows[0:1, R_CELL + 31:R_CELL + 34], gwl_s[0:1, :],
                                       cell(5), None, ALU.mult)
            tg7 = P.op("vector", g7, waits=[tg6], inc="vv")
            def g8(b):
                b.tensor_add(rows[0:1, R_CELL + 2:R_CELL + 5],
                             rows[0:1, R_CELL + 2:R_CELL + 5],
                             rows[0:1, R_CELL + 28:R_CELL + 31])
                return b.tensor_add(rows[0:1, R_CELL + 6:R_CELL + 9],
                                    rows[0:1, R_CELL + 31:R_CELL + 34], gbr_s[0:1, :])
            tg8 = P.op("vector", g8, waits=[tg7], inc="vv")
            tg9 = P.op("vector",
                       lambda b: b.tensor_add(rows[0:1, R_CELL + 2:R_CELL + 5],
                                              rows[0:1, R_CELL + 2:R_CELL + 5],
                                              rows[0:1, R_CELL + 31:R_CELL + 34]),
                       waits=[tg8], inc="vv")
            t_gate0 = P.op("vector",
                           lambda b: b.tensor_add(rows[0:1, R_CELL + 2:R_CELL + 5],
                                                  rows[0:1, R_CELL + 2:R_CELL + 5],
                                                  gbr_s[0:1, :]),
                           waits=[tg9], inc="vv")

            def top_ind(out_off, waits):
                def fa(b):
                    L = lambda i: rows[0:1, R_CELL + 2 + i:R_CELL + 3 + i]
                    b.tensor_tensor(cell(15), L(0), L(1), ALU.is_ge)
                    b.tensor_tensor(cell(16), L(0), L(2), ALU.is_ge)
                    return b.tensor_tensor(cell(21), L(1), L(2), ALU.is_ge)
                t1 = P.op("vector", fa, waits=waits, inc="vv")
                def fb(b):
                    O = lambda i: rows[0:1, out_off + i:out_off + i + 1]
                    b.tensor_mul(O(0), cell(15), cell(16))
                    b.tensor_scalar(cell(23), cell(15), -1.0, 1.0, ALU.mult, ALU.add)
                    b.tensor_scalar(cell(24), cell(16), -1.0, 1.0, ALU.mult, ALU.add)
                    return b.tensor_scalar(cell(27), cell(21), -1.0, 1.0, ALU.mult, ALU.add)
                t2 = P.op("vector", fb, waits=[t1], inc="vv")
                def fc(b):
                    O = lambda i: rows[0:1, out_off + i:out_off + i + 1]
                    b.tensor_mul(O(1), cell(23), cell(21))
                    return b.tensor_mul(O(2), cell(24), cell(27))
                return P.op("vector", fc, waits=[t2], inc="vv")

            t_i1 = top_ind(R_CELL + 9, [t_gate0])
            t_m1 = P.op("vector",
                        lambda b: b.tensor_scalar(rows[0:1, R_CELL + 34:R_CELL + 37],
                                                  rows[0:1, R_CELL + 9:R_CELL + 12],
                                                  -1e30, None, ALU.mult),
                        waits=[t_i1], inc="vv")
            t_msk = P.op("vector",
                         lambda b: b.tensor_add(rows[0:1, R_CELL + 2:R_CELL + 5],
                                                rows[0:1, R_CELL + 2:R_CELL + 5],
                                                rows[0:1, R_CELL + 34:R_CELL + 37]),
                         waits=[t_m1], inc="vv")
            t_i2 = top_ind(R_CELL + 12, [t_msk])

            PHI = {0: AF.Tanh, 1: AF.Gelu, 2: AF.Silu}
            for t in range(NT):
                tlh = P.op("sync",
                           lambda b, t=t: b.dma_start(out=bigbf[:, :], in_=d_hs[t]),
                           waits=[last.get("hfree0"), last.get("hfree1")] +
                                 (joinA if t == 0 else []) +
                                 ([P.tok("sh")] if t == 0 else []),
                           inc="hld", amt=16)
                t_h = P.tok("hld")

                def fg(b):
                    for kc in range(NKC):
                        i = b.matmul(rowsC[0:3, :], gwb_s[:, kc * 3:(kc + 1) * 3],
                                     bigbf[:, kc * TT:(kc + 1) * TT],
                                     start=(kc == 0), stop=(kc == NKC - 1))
                    return i
                t_gmm = P.op("tensor", fg,
                             waits=[t_h, t_casts, last.get("rowsC_rd")], inc="mm")
                t_gev = P.op("vector",
                             lambda b: b.tensor_copy(rows2[0:3, :], rowsC[0:3, :]),
                             waits=[t_gmm, last.get("rows2f")], inc="vv")
                last["rowsC_rd"] = t_gev
                def fmv(b):
                    r = []
                    for e in range(3):
                        r.append((b.dma_start(out=rows[0:1, R_L + e * TT:R_L + (e + 1) * TT],
                                              in_=rows2[e:e + 1, :]), 16))
                    return r
                t_gmv = P.op("gpsimd", fmv, waits=[t_gev, last.get("rowsL_rd")],
                             inc="smis", amt=48)
                t_gmvt = P.tok("smis")

                def fs(b):
                    LG = lambda e: row(R_L + e * TT)
                    b.tensor_scalar(LG(0), LG(0), cell(6), None, ALU.add)
                    b.tensor_scalar(LG(1), LG(1), cell(7), None, ALU.add)
                    b.tensor_scalar(LG(2), LG(2), cell(8), None, ALU.add)
                    b.tensor_max(row(R_TMP), LG(0), LG(1))
                    b.tensor_max(row(R_TMP), row(R_TMP), LG(2))
                    b.tensor_tensor(row(R_TMP2), LG(0), LG(1), ALU.min)
                    b.tensor_tensor(row(R_TMP2), row(R_TMP2), LG(2), ALU.min)
                    b.tensor_sub(row(R_E + 0 * TT), LG(0), row(R_TMP))
                    b.tensor_sub(row(R_E + 1 * TT), LG(1), row(R_TMP))
                    b.tensor_sub(row(R_E + 2 * TT), LG(2), row(R_TMP))
                    return b.tensor_sub(row(R_E + 3 * TT), row(R_TMP2), row(R_TMP))
                t_sm1 = P.op("vector", fs, waits=[t_gmvt, t_gate0, t_i2], inc="vv")
                t_sm2 = P.op("scalar",
                             lambda b: b.activation(rows[0:1, R_E:R_E + 4 * TT],
                                                    rows[0:1, R_E:R_E + 4 * TT], AF.Exp),
                             waits=[t_sm1], inc="ss")
                def fs2(b):
                    E = lambda i: row(R_E + i * TT)
                    b.tensor_add(row(R_S), E(0), E(1))
                    b.tensor_add(row(R_S), row(R_S), E(2))
                    b.reciprocal(row(R_RECS), row(R_S))
                    b.tensor_sub(row(R_A2), row(R_S), E(3))
                    b.tensor_mul(row(R_A2), row(R_A2), row(R_RECS))
                    b.tensor_scalar_add(row(R_A2), row(R_A2), 1e-8)
                    b.reciprocal(row(R_A2), row(R_A2))
                    b.tensor_mul(row(R_WTOP), row(R_RECS), row(R_A2))
                    b.tensor_scalar_add(row(R_W2ND), row(R_S), -1.0)
                    b.tensor_sub(row(R_W2ND), row(R_W2ND), E(3))
                    b.tensor_mul(row(R_W2ND), row(R_W2ND), row(R_RECS))
                    b.tensor_mul(row(R_W2ND), row(R_W2ND), row(R_A2))
                    r = None
                    for e in range(3):
                        b.tensor_scalar(row(R_TMP), row(R_WTOP), cell(9 + e), None, ALU.mult)
                        b.tensor_scalar(row(R_TMP2), row(R_W2ND), cell(12 + e), None, ALU.mult)
                        r = b.tensor_add(row(R_L + e * TT), row(R_TMP), row(R_TMP2))
                    return r
                t_coefr = P.op("vector", fs2, waits=[t_sm2], inc="vv")
                last["rowsL_rd"] = t_coefr
                def fc3(b):
                    r = []
                    for e in range(3):
                        r.append((b.dma_start(out=coef3[e:e + 1, :],
                                              in_=rows[0:1, R_L + e * TT:R_L + (e + 1) * TT]),
                                  16))
                    return r
                t_c3 = P.op("gpsimd", fc3,
                            waits=[t_coefr, last.get("coef3_rd")], inc="smis", amt=48)
                t_c3t = P.tok("smis")
                tcb = {}
                for e in range(3):
                    tcb[e] = bcast2(row(R_L + e * TT), coefb[:, e * TT:(e + 1) * TT],
                                    [t_coefr, last.get(("coefb", e))])

                moe_ev = {}
                mstat_last = None
                SLOT_SEMS = ("wa0", "wa1", "wb0", "wb1")
                if ("wu",) not in last:
                    last[("wu",)] = 0
                def wload(eng, dsrc, extra_waits):
                    u = last[("wu",)]
                    last[("wu",)] = u + 1
                    s = u % 4
                    sem = SLOT_SEMS[s]
                    P.op(eng,
                         lambda b, dsrc=dsrc, s=s: b.dma_start(
                             out=ws_s[:, s * 2048:(s + 1) * 2048], in_=dsrc),
                         waits=[last.get(("wslot", s))] + list(extra_waits),
                         inc=sem, amt=16)
                    return P.tok(sem), s
                for e in range(3):
                    act_tok = {}
                    for fc in range(0, NKC, 2):
                        ew = [t_conv[e], joinA[0]] if (t == 0 and fc == 0) else []
                        la0, sa0 = wload("sync", d_w1bf[e, fc][:, 0:2048], ew)
                        la1, sa1 = wload("sync", d_w1bf[e, fc][:, 2048:4096], [])
                        lb0, sb0 = wload("sync", d_w1bf[e, fc + 1][:, 0:2048], [])
                        lb1, sb1 = wload("sync", d_w1bf[e, fc + 1][:, 2048:4096], [])
                        wmm = [la0, la1, lb0, lb1, t_h,
                               last.get(("pre", 0)), last.get(("pre", 1))]
                        def fp(b, sa0=sa0, sa1=sa1, sb0=sb0, sb1=sb1):
                            for pp, s0, s1 in ((pre0, sa0, sa1), (pre1, sb0, sb1)):
                                for k in range(16):
                                    b.matmul(pp[:, :],
                                             ws_s[:, s0 * 2048 + k * 128: s0 * 2048 + (k + 1) * 128],
                                             bigbf[:, k * TT:(k + 1) * TT],
                                             start=(k == 0), stop=False)
                                for k in range(16):
                                    kc = 16 + k
                                    i = b.matmul(pp[:, :],
                                                 ws_s[:, s1 * 2048 + k * 128: s1 * 2048 + (k + 1) * 128],
                                                 bigbf[:, kc * TT:(kc + 1) * TT],
                                                 start=False, stop=(k == 15))
                            return i
                        tp = P.op("tensor", fp, waits=wmm, inc="mm")
                        for s_ in (sa0, sa1, sb0, sb1):
                            last[("wslot", s_)] = tp
                        def fact(b, e=e, fc=fc):
                            b.activation(act_s[:, fc * TT:(fc + 1) * TT], pre0[:, :],
                                         PHI[e], bias=b1c_s[:, e * 32 + fc:e * 32 + fc + 1])
                            return b.activation(act_s[:, (fc + 1) * TT:(fc + 2) * TT],
                                                pre1[:, :], PHI[e],
                                                bias=b1c_s[:, e * 32 + fc + 1:e * 32 + fc + 2])
                        ta = P.op("scalar", fact,
                                  waits=[tp, t_lpr, last.get(("actw", fc)),
                                         last.get(("actw", fc + 1))], inc="ss")
                        last[("pre", 0)] = ta
                        last[("pre", 1)] = ta
                        def fsc(b, e=e, fc=fc):
                            b.tensor_mul(act_s[:, fc * TT:(fc + 1) * TT],
                                         act_s[:, fc * TT:(fc + 1) * TT],
                                         coefb[:, e * TT:(e + 1) * TT])
                            return b.tensor_mul(act_s[:, (fc + 1) * TT:(fc + 2) * TT],
                                                act_s[:, (fc + 1) * TT:(fc + 2) * TT],
                                                coefb[:, e * TT:(e + 1) * TT])
                        tsc = P.op("vector", fsc, waits=[ta, tcb[e]], inc="vv")
                        act_tok[fc] = tsc
                        act_tok[fc + 1] = tsc
                    act_all = [act_tok[fc] for fc in range(0, NKC, 2)]
                    last[("coefb", e)] = act_all[-1]
                    for dc in range(0, NKC, 2):
                        ew = [t_conv[3 + e]] if (t == 0 and dc == 0) else []
                        la0, sa0 = wload("scalar", d_w2bf[e, dc][:, 0:2048], ew)
                        la1, sa1 = wload("scalar", d_w2bf[e, dc][:, 2048:4096], [])
                        lb0, sb0 = wload("scalar", d_w2bf[e, dc + 1][:, 0:2048], [])
                        lb1, sb1 = wload("scalar", d_w2bf[e, dc + 1][:, 2048:4096], [])
                        wmm = [la0, la1, lb0, lb1, last.get(("out",))]
                        if dc == 0:
                            wmm += act_all
                        if e == 2:
                            wmm.append(t_c3t)
                        def fo(b, sa0=sa0, sa1=sa1, sb0=sb0, sb1=sb1, e=e, dc=dc):
                            for po, s0, s1, d_ in ((out0, sa0, sa1, dc), (out1, sb0, sb1, dc + 1)):
                                for k in range(16):
                                    b.matmul(po[:, :],
                                             ws_s[:, s0 * 2048 + k * 128: s0 * 2048 + (k + 1) * 128],
                                             act_s[:, k * TT:(k + 1) * TT],
                                             start=(k == 0), stop=False)
                                for k in range(16):
                                    fc = 16 + k
                                    lastmm = (k == 15 and e != 2)
                                    i = b.matmul(po[:, :],
                                                 ws_s[:, s1 * 2048 + k * 128: s1 * 2048 + (k + 1) * 128],
                                                 act_s[:, fc * TT:(fc + 1) * TT],
                                                 start=False, stop=lastmm)
                                if e == 2:
                                    i = b.matmul(po[:, :], b2s_s[0:3, d_ * 128:(d_ + 1) * 128],
                                                 coef3[0:3, :], start=False, stop=True)
                            return i
                        tmo = P.op("tensor", fo, waits=wmm + [t_coefr], inc="mm")
                        for s_ in (sa0, sa1, sb0, sb1):
                            last[("wslot", s_)] = tmo
                        if e == 0:
                            def fev(b, dc=dc):
                                b.tensor_copy(moeacc[:, dc * TT:(dc + 1) * TT], out0[:, :])
                                return b.tensor_copy(moeacc[:, (dc + 1) * TT:(dc + 2) * TT],
                                                     out1[:, :])
                        else:
                            def fev(b, dc=dc):
                                b.tensor_add(moeacc[:, dc * TT:(dc + 1) * TT], out0[:, :],
                                             moeacc[:, dc * TT:(dc + 1) * TT])
                                return b.tensor_add(moeacc[:, (dc + 1) * TT:(dc + 2) * TT],
                                                    out1[:, :],
                                                    moeacc[:, (dc + 1) * TT:(dc + 2) * TT])
                        tev = P.op("vector", fev,
                                   waits=[tmo, last.get(("macc", dc)),
                                          last.get(("macc", dc + 1))], inc="vv")
                        last[("out",)] = tev
                        last[("macc", dc)] = tev
                        last[("macc", dc + 1)] = tev
                        last[("lastout",)] = tmo
                        if e == 2:
                            moe_ev[dc] = tev
                            moe_ev[dc + 1] = tev
                            def fst(b, dc=dc):
                                b.activation(sqc2[:, 0:TT],
                                             moeacc[:, dc * TT:(dc + 1) * TT], AF.Copy)
                                b.activation(sqc2[:, TT:2 * TT],
                                             moeacc[:, (dc + 1) * TT:(dc + 2) * TT], AF.Copy)
                                b.activation(sqpp[:, 0:TT],
                                             moeacc[:, dc * TT:(dc + 1) * TT], AF.Square)
                                return b.activation(sqpp[:, TT:2 * TT],
                                                    moeacc[:, (dc + 1) * TT:(dc + 2) * TT],
                                                    AF.Square)
                            tss_ = P.op("scalar", fst,
                                        waits=[tev, last.get(("sqc", 0))], inc="ss")
                            def fm1(b, dc=dc):
                                b.matmul(rowsC[0:1, :], onesb_s[:, 0:1], sqc2[:, 0:TT],
                                         start=(dc == 0), stop=False)
                                return b.matmul(rowsC[0:1, :], onesb_s[:, 0:1],
                                                sqc2[:, TT:2 * TT],
                                                start=False, stop=(dc + 1 == NKC - 1))
                            tm1 = P.op("tensor", fm1,
                                       waits=[tss_] +
                                             ([t_gev, last.get("rowsC_rd")] if dc == 0 else []),
                                       inc="mm")
                            def fm2(b, dc=dc):
                                b.matmul(statF2[0:1, :], onesb_s[:, 0:1], sqpp[:, 0:TT],
                                         start=(dc == 0), stop=False)
                                return b.matmul(statF2[0:1, :], onesb_s[:, 0:1],
                                                sqpp[:, TT:2 * TT],
                                                start=False, stop=(dc + 1 == NKC - 1))
                            tm2 = P.op("tensor", fm2, waits=[], inc="mm")
                            last[("sqc", 0)] = tm2
                            mstat_last = tm2
                    for fc in range(NKC):
                        last[("actw", fc)] = last[("lastout",)]

                def f(b):
                    b.tensor_scalar_mul(row(R_M3), rowsC[0:1, :], 1.0 / HID)
                    b.tensor_scalar_mul(row(R_TMP), statF2[0:1, :], 1.0 / HID)
                    b.tensor_tensor(row(R_TMP2), row(R_M3), row(R_M3), ALU.mult)
                    b.tensor_sub(row(R_TMP), row(R_TMP), row(R_TMP2))
                    b.tensor_scalar_add(row(R_TMP), row(R_TMP), EPS_LN)
                    b.reciprocal(row(R_TMP), row(R_TMP))
                    return b.tensor_scalar_mul(row(R_NEGM3), row(R_M3), -1.0)
                t_r3a = P.op("vector", f, waits=[mstat_last], inc="vv")
                t_r3b = P.op("scalar",
                             lambda b: b.activation(row(R_RSTD3), row(R_TMP), AF.Sqrt),
                             waits=[t_r3a], inc="ss")
                last["rowsC_rd"] = t_r3a
                t_bc_negm3 = bcast2(row(R_NEGM3), bc0[:, :], [t_r3a])
                t_bc_rstd3 = bcast2(row(R_RSTD3), bc1[:, :], [t_r3b])

                for dc in range(NKC):
                    par = dc % 2
                    slot = dc % 2
                    sem = "xc%d" % slot
                    tl = P.op("scalar",
                              lambda b, t=t, dc=dc, slot=slot: b.dma_start(
                                  out=x1c_s[:, slot * TT:(slot + 1) * TT],
                                  in_=d_x1s[t][:, dc * TT:(dc + 1) * TT]),
                              waits=[last.get(("x1cl", slot))] +
                                    (joinA if t == 0 and dc < 2 else []),
                              inc=sem, amt=16)
                    t_l = P.tok(sem)
                    t1 = P.op("vector",
                              lambda b, dc=dc, par=par: b.tensor_add(
                                  tmpa[:, par * TT:(par + 1) * TT],
                                  moeacc[:, dc * TT:(dc + 1) * TT], bc0[:, :]),
                              waits=[t_bc_negm3, moe_ev[dc], last.get(("tmpaC", par))],
                              inc="vv")
                    last[("macc", dc)] = t1
                    t2 = P.op("vector",
                              lambda b, par=par: b.tensor_mul(
                                  tmpa[:, par * TT:(par + 1) * TT],
                                  tmpa[:, par * TT:(par + 1) * TT], bc1[:, :]),
                              waits=[t1, t_bc_rstd3], inc="vv")
                    t3 = P.op("vector",
                              lambda b, par=par, slot=slot: b.tensor_add(
                                  tmpa[:, par * TT:(par + 1) * TT],
                                  tmpa[:, par * TT:(par + 1) * TT],
                                  x1c_s[:, slot * TT:(slot + 1) * TT]),
                              waits=[t_l], inc="vv")
                    last[("tmpaC", par)] = t3
                    last[("x1cl", slot)] = t3
                    pp = tpo0 if dc % 2 == 0 else tpo1
                    def ftp(b, pp=pp, par=par):
                        for tc in range(4):
                            i = b.transpose(pp[:, tc * 128:(tc + 1) * 128],
                                            tmpa[:, par * TT + tc * 128: par * TT + (tc + 1) * 128],
                                            ident_s[:, :])
                        return i
                    tmm = P.op("tensor", ftp,
                               waits=[t3, last.get(("tpoev", dc % 2))] +
                                     ([last.get("bcfree")] if dc % 2 == 1 else []),
                               inc="mm")
                    tev = P.op("vector",
                               lambda b, pp=pp: b.tensor_copy(x2t_s[:, :], pp[:, :]),
                               waits=[tmm, last.get("x2t")], inc="vv")
                    last[("tpoev", dc % 2)] = tev
                    tso = P.op("scalar",
                               lambda b, t=t, dc=dc: b.dma_start(
                                   out=d_out[t * TT:(t + 1) * TT, dc * 128:(dc + 1) * 128]
                                   .rearrange("(tc p) d -> tc p d", p=128).transpose([1, 0, 2]),
                                   in_=x2t_s[:, :].rearrange("p (tc d) -> p tc d", tc=4)),
                               waits=[tev], inc="sout", amt=16)
                    last["x2t"] = tso
                last["hfree0"] = last[("pre", 0)]
                last["hfree1"] = last[("pre", 1)]
                last["coef3_rd"] = last[("lastout",)]
                last["rows2f"] = t_gmvt
                last["rowsL_rd"] = last[("lastout",)]

            if DEBUG_TAPS:
                def ftap3(b):
                    r = []
                    r.append((b.dma_start(out=d_moedbg, in_=moeacc[:, :]), 16))
                    r.append((b.dma_start(out=d_rowdbg2, in_=rows[0:1, :]), 16))
                    r.append((b.dma_start(out=d_cfdbg, in_=coefb[:, :]), 16))
                    return r
                P.op("gpsimd", ftap3,
                     waits=[P.tok("vv"), P.tok("ss"), P.tok("mm")], inc="smis", amt=48)
            P.op("sync", lambda b: b.nop(), waits=[P.tok("sout"), P.tok("smis")])
            P.flush(blk, sems)

    return nc


_NC_CACHE = None


def kernel(**inputs):
    global _NC_CACHE
    in_maps = host_prep(inputs)
    if _NC_CACHE is None:
        _NC_CACHE = build_nc()
    nc = _NC_CACHE
    res = run_bass_kernel_spmd(nc, in_maps, core_ids=list(range(NCORES)))
    out = np.concatenate([res.results[c]["out"] for c in range(NCORES)], axis=0)
    return out.reshape(2, S_FULL, HID)


# version 05
# speedup vs baseline: 1.2183x; 1.2183x over previous
import numpy as np
from contextlib import ExitStack

import concourse.bass as bass
import concourse.mybir as mybir
from concourse.bass_utils import run_bass_kernel_spmd

F32 = mybir.dt.float32
BF16 = mybir.dt.bfloat16
AF = mybir.ActivationFunctionType
ALU = mybir.AluOpType
AX = mybir.AxisListType

NCORES = 8
T_CORE = 2048
NT = 4
TT = 512
HID = 4096
NKC = 32
S_FULL = 8192
VIS_LEN = 2457
EPS_LN = 1e-5
EPS_ATTN = 1e-6

ENGS = ("sync", "tensor", "vector", "scalar", "gpsimd")


class Prog:

    def __init__(self):
        self.pend = {e: [] for e in ENGS}
        self.cnt = {}
        self.waited = {e: {} for e in ENGS}

    def tok(self, sem):
        return (sem, self.cnt.get(sem, 0))

    def op(self, eng, fn, waits=(), inc=None, amt=1):
        if inc is not None:
            self.cnt[inc] = self.cnt.get(inc, 0) + amt
        ws = [w for w in waits if w is not None and w[0] is not None and w[1] > 0]
        self.pend[eng].append((ws, fn, inc, amt))
        return (inc, self.cnt[inc]) if inc is not None else None

    def flush(self, block, sems):
        for eng in ENGS:
            ops = self.pend[eng]
            if not ops:
                continue
            self.pend[eng] = []
            waited = self.waited[eng]

            def body(b, ops=ops, waited=waited):
                for waits, fn, inc, amt in ops:
                    for s, v in waits:
                        if v > waited.get(s, 0):
                            b.wait_ge(sems[s], v)
                            waited[s] = v
                    r = fn(b)
                    if inc is not None:
                        if isinstance(r, list):
                            tot = 0
                            for inst, a in r:
                                inst.then_inc(sems[inc], a)
                                tot += a
                            assert tot == amt, (tot, amt)
                        else:
                            r.then_inc(sems[inc], amt)
            getattr(block, eng)(body)


def host_prep(inputs):
    X = np.ascontiguousarray(
        np.asarray(inputs["L0_attn_hidden_states"], np.float32).reshape(2 * S_FULL, HID))
    Wq = np.asarray(inputs["L0_attn_Wq"], np.float32)
    sel = np.concatenate([np.arange(h * 128, (h + 1) * 128) for h in range(0, 32, 4)])
    WqselT = np.ascontiguousarray(Wq[sel, :].T)
    wq_host = np.ascontiguousarray(
        WqselT.reshape(32, 128, 8, 128).transpose(2, 1, 0, 3).reshape(8, 128, 32 * 128))
    WoT = np.ascontiguousarray(np.asarray(inputs["L0_attn_Wo"], np.float32).T)
    w1 = np.asarray(inputs["L0_ffn_w1"], np.float32)
    w2 = np.asarray(inputs["L0_ffn_w2"], np.float32)
    w1_host = np.ascontiguousarray(
        w1.reshape(3, 32, 128, 32, 128).transpose(0, 3, 2, 1, 4).reshape(3, 32, 128, 32 * 128))
    w2_host = np.ascontiguousarray(
        w2.reshape(3, 32, 128, 32, 128).transpose(0, 3, 2, 1, 4).reshape(3, 32, 128, 32 * 128))
    b1c = np.ascontiguousarray(
        np.asarray(inputs["L0_ffn_b1"], np.float32).reshape(3, 32, 128).transpose(2, 0, 1)
        .reshape(128, 96))
    b2m = np.ascontiguousarray(np.asarray(inputs["L0_ffn_b2"], np.float32))
    gw = np.asarray(inputs["L0_ffn_gate_w"], np.float32)
    gwm = np.ascontiguousarray(gw[:HID].reshape(32, 128, 3).transpose(1, 0, 2).reshape(128, 96))
    gwl = np.ascontiguousarray(gw[HID].reshape(1, 3))
    gbr = np.ascontiguousarray(np.asarray(inputs["L0_ffn_gate_b"], np.float32).reshape(1, 3))
    bqr = np.ascontiguousarray(np.asarray(inputs["L0_attn_bq"], np.float32)[sel].reshape(1, 1024))
    Mh = np.ascontiguousarray(
        np.asarray(inputs["L0_attn_M"], np.float32).transpose(1, 0, 2).reshape(128, 8 * 128))
    zc = np.ascontiguousarray(np.asarray(inputs["L0_attn_z"], np.float32).T)
    ident = np.eye(128, dtype=np.float32)
    onesr = np.ones((1, TT), np.float32)

    shared = dict(wq=wq_host, wot=WoT, w1=w1_host, w2=w2_host, b1c=b1c, b2m=b2m,
                  gwm=gwm, gwl=gwl, gbr=gbr, bqr=bqr, mmat=Mh, zc=zc,
                  ident=ident, onesr=onesr)
    in_maps = []
    for c in range(NCORES):
        tok = np.arange(c * T_CORE, (c + 1) * T_CORE) % S_FULL
        maskv = (tok < VIS_LEN).astype(np.float32).reshape(1, T_CORE)
        meta = np.zeros((1, 8), np.float32)
        meta[0, 0] = 1.0 if c < 4 else 0.0
        meta[0, 1] = 0.0 if c < 4 else 1.0
        meta[0, 2] = 1.0 if c % 4 == 0 else 0.0
        m = dict(shared)
        m["x"] = np.ascontiguousarray(X[c * T_CORE:(c + 1) * T_CORE])
        m["maskv"] = maskv
        m["meta"] = meta
        in_maps.append(m)
    return in_maps


R_M0, R_RSTD0, R_NEGM0, R_STD0 = 0, 512, 1024, 1536
R_MA, R_RSTDA, R_NEGMA = 2048, 2560, 3072
R_RSTD2 = 3584
R_TMP, R_TMP2 = 4096, 4608
R_DEN = 5120
R_SABS = 5632
R_CELL = 6144
R_L = 0
R_E = 1536
R_S, R_RECS, R_A2, R_WTOP, R_W2ND = 3584, 4096, 4608, 5120, 5632
R_M3, R_RSTD3, R_NEGM3 = 5120, 5632, 4608
ROWS_LEN = 6208


BUILD_STAGE = "all"
DEBUG_TAPS = False


def build_nc(stage=None):
    stage = stage or BUILD_STAGE
    nc = bass.Bass("TRN2", target_bir_lowering=False)
    d_x = nc.dram_tensor("x", [T_CORE, HID], F32, kind="ExternalInput").ap()
    d_wq = nc.dram_tensor("wq", [8, 128, 32 * 128], F32, kind="ExternalInput").ap()
    d_wot = nc.dram_tensor("wot", [HID, HID], F32, kind="ExternalInput").ap()
    d_w1 = nc.dram_tensor("w1", [3, 32, 128, 32 * 128], F32, kind="ExternalInput").ap()
    d_w2 = nc.dram_tensor("w2", [3, 32, 128, 32 * 128], F32, kind="ExternalInput").ap()
    d_b1c = nc.dram_tensor("b1c", [128, 96], F32, kind="ExternalInput").ap()
    d_b2m = nc.dram_tensor("b2m", [3, HID], F32, kind="ExternalInput").ap()
    d_gwm = nc.dram_tensor("gwm", [128, 96], F32, kind="ExternalInput").ap()
    d_gwl = nc.dram_tensor("gwl", [1, 3], F32, kind="ExternalInput").ap()
    d_gbr = nc.dram_tensor("gbr", [1, 3], F32, kind="ExternalInput").ap()
    d_bqr = nc.dram_tensor("bqr", [1, 1024], F32, kind="ExternalInput").ap()
    d_m = nc.dram_tensor("mmat", [128, 8 * 128], F32, kind="ExternalInput").ap()
    d_zc = nc.dram_tensor("zc", [128, 8], F32, kind="ExternalInput").ap()
    d_ident = nc.dram_tensor("ident", [128, 128], F32, kind="ExternalInput").ap()
    d_onesr = nc.dram_tensor("onesr", [1, TT], F32, kind="ExternalInput").ap()
    d_maskv = nc.dram_tensor("maskv", [1, T_CORE], F32, kind="ExternalInput").ap()
    d_meta = nc.dram_tensor("meta", [1, 8], F32, kind="ExternalInput").ap()

    d_out = nc.dram_tensor("out", [T_CORE, HID], F32, kind="ExternalOutput").ap()

    d_wqbf = nc.dram_tensor("wqbf", [8, 128, 32 * 128], BF16, kind="Internal").ap()
    d_wobf = nc.dram_tensor("wobf", [32, 128, 8 * 128], BF16, kind="Internal").ap()
    d_w1bf = nc.dram_tensor("w1bf", [3, 32, 128, 32 * 128], BF16, kind="Internal").ap()
    d_w2bf = nc.dram_tensor("w2bf", [3, 32, 128, 32 * 128], BF16, kind="Internal").ap()
    d_x1s = nc.dram_tensor("x1s", [NT, 128, NKC * TT], F32, kind="Internal").ap()
    d_hs = nc.dram_tensor("hs", [NT, 128, NKC * TT], BF16, kind="Internal").ap()
    d_ccin = nc.dram_tensor("ccin", [1, 32], F32, kind="Internal").ap()
    d_ccout = nc.dram_tensor("ccout", [1, 32], F32, kind="Internal").ap()
    if DEBUG_TAPS:
        d_x1dbg = nc.dram_tensor("x1dbg", [NT, 128, NKC * TT], F32, kind="ExternalOutput").ap()
        d_hdbg = nc.dram_tensor("hdbg", [NT, 128, NKC * TT], F32, kind="ExternalOutput").ap()
        d_ardbg = nc.dram_tensor("ardbg", [1, 32], F32, kind="ExternalOutput").ap()
        d_rowdbg = nc.dram_tensor("rowdbg", [1, ROWS_LEN], F32, kind="ExternalOutput").ap()
        d_atdbg = nc.dram_tensor("atdbg", [128, NKC * TT], BF16, kind="ExternalOutput").ap()
        d_rtdbg = nc.dram_tensor("rtdbg", [128, 8 * TT], BF16, kind="ExternalOutput").ap()
        d_moedbg = nc.dram_tensor("moedbg", [128, NKC * TT], F32, kind="ExternalOutput").ap()
        d_rowdbg2 = nc.dram_tensor("rowdbg2", [1, ROWS_LEN], F32, kind="ExternalOutput").ap()
        d_cfdbg = nc.dram_tensor("cfdbg", [128, 3 * TT], BF16, kind="ExternalOutput").ap()

    P = Prog()
    SEMS = ["mm", "vv", "ss", "gg",
            "xin", "wq0", "wq1", "wo0", "wo1",
            "wa0", "wa1", "wb0", "wb1", "xc0", "xc1", "xc2", "xc3", "hld",
            "sx1", "sh", "sout", "spr", "smis", "lpr",
            "cv10", "cv11", "cv12", "cv20", "cv21", "cv22"]

    with ExitStack() as outer:
        sems = {s: outer.enter_context(nc.semaphore(s)) for s in SEMS}
        blk = outer.enter_context(nc.Block())
        sb = lambda n, shp, dt: outer.enter_context(nc.sbuf_tensor(n, shp, dt))

        bigbf = sb("bigbf", [128, NKC * TT], BF16)
        rows = sb("rows", [1, ROWS_LEN], F32)
        rows2 = sb("rows2", [3, TT], F32)
        bc0 = sb("bc0", [128, TT], F32)
        bc1 = sb("bc1", [128, TT], F32)
        tmpa = sb("tmpa", [128, 2 * TT], F32)
        sqpp = sb("sqpp", [128, 2 * TT], BF16)
        sqc2 = sb("sqc2", [128, 2 * TT], BF16)
        ident_s = sb("ident_s", [128, 128], F32)
        onesr_s = sb("onesr_s", [1, TT], F32)
        onesb_s = sb("onesb_s", [128, 1], BF16)
        gwm_s = sb("gwm_s", [128, 96], F32)
        gwb_s = sb("gwb_s", [128, 96], BF16)
        b1c_s = sb("b1c_s", [128, 96], F32)
        meta_s = sb("meta_s", [1, 8], F32)
        gwl_s = sb("gwl_s", [1, 3], F32)
        gbr_s = sb("gbr_s", [1, 3], F32)
        arst_s = sb("arst_s", [1, 32], F32)
        arout_s = sb("arout_s", [1, 32], F32)
        x1c_s = sb("x1c_s", [128, 2 * TT], F32)
        b2s_s = sb("b2s_s", [3, HID], BF16)
        colw = sb("colw", [128, 4], F32)

        def cell(i):
            return rows[0:1, R_CELL + i:R_CELL + i + 1]

        def row(off, n=TT):
            return rows[0:1, off:off + n]

        last = {}

        with ExitStack() as sA:
            sbA = lambda n, shp, dt: sA.enter_context(nc.sbuf_tensor(n, shp, dt))
            mz_s = sbA("mz_s", [128, 8 * 128 + 8], BF16)
            bqr_s = sbA("bqr_s", [1, 1024], F32)
            maskv_s = sbA("maskv_s", [1, T_CORE], F32)
            wsq_s = sbA("wsq_s", [1, 1024], F32)
            bc2 = sbA("bc2", [128, TT], F32)
            bc3 = sbA("bc3", [128, TT], F32)
            tmpb = sbA("tmpb", [128, 2 * TT], F32)

            def bcast(src_row, dst_tile, waits):
                tm = P.op("tensor",
                          lambda b, src_row=src_row: b.matmul(
                              bcH[:, :], onesr_s[0:1, 0:128], src_row,
                              start=True, stop=True),
                          waits=list(waits) + [last.get("bcfree"), P.tok("vv")],
                          inc="mm")
                te = P.op("scalar",
                          lambda b, dst_tile=dst_tile: b.activation(
                              dst_tile, bcH[:, :], AF.Copy),
                          waits=[tm], inc="ss")
                last["bcfree"] = te
                return te

            with ExitStack() as sP:
                wof_s = sP.enter_context(nc.sbuf_tensor("wof_s", [128, 4 * HID], F32))
                wqp_s = sP.enter_context(nc.sbuf_tensor("wqp_s", [128, 2 * HID], BF16))
                wsq_ps = sP.enter_context(nc.psum_tensor("wsq_ps", [1, 128], F32))

                def f(b):
                    r = []
                    for dst, src in ((ident_s[:, :], d_ident), (onesr_s[:, :], d_onesr),
                                     (bqr_s[:, :], d_bqr), (b1c_s[:, :], d_b1c),
                                     (gwm_s[:, :], d_gwm), (maskv_s[:, :], d_maskv),
                                     (meta_s[:, :], d_meta), (gwl_s[:, :], d_gwl),
                                     (gbr_s[:, :], d_gbr)):
                        r.append((b.dma_start(out=dst, in_=src), 16))
                    return r
                P.op("sync", f, inc="lpr", amt=16 * 9)
                t_lpr = P.tok("lpr")

                def f(b):
                    r = []
                    r.append((b.dma_start(out=mz_s[:, 0:8 * 128], in_=d_m), 16))
                    r.append((b.dma_start(out=mz_s[:, 8 * 128:8 * 128 + 8], in_=d_zc), 16))
                    r.append((b.dma_start(out=gwb_s[:, :], in_=d_gwm), 16))
                    r.append((b.dma_start(out=b2s_s[:, :], in_=d_b2m), 16))
                    return r
                if stage != "prep0":
                    P.op("gpsimd", f, inc="smis", amt=16 * 4)
                t_casts = P.tok("smis")
                t_onesb = P.op("vector", lambda b: b.memset(onesb_s[:, :], 1.0), inc="vv")

                for e in (range(3) if stage not in ("prep0", "prep1") else []):
                    def f1(b, e=e):
                        r = []
                        for fc in range(NKC):
                            r.append((b.dma_start(out=d_w1bf[e, fc], in_=d_w1[e, fc]), 16))
                        return r
                    P.op("gpsimd", f1, inc="cv1%d" % e, amt=16 * NKC)
                    def f2(b, e=e):
                        r = []
                        for dc in range(NKC):
                            r.append((b.dma_start(out=d_w2bf[e, dc], in_=d_w2[e, dc]), 16))
                        return r
                    P.op("gpsimd", f2, inc="cv2%d" % e, amt=16 * NKC)
                t_conv = [P.tok("cv1%d" % e) for e in range(3)] + \
                         [P.tok("cv2%d" % e) for e in range(3)]

                wq_wb = {}
                for fc in (range(8) if stage not in ("prep0", "prep1", "prep2") else []):
                    par = fc % 2
                    off = par * HID
                    tl = P.op("gpsimd",
                              lambda b, fc=fc, off=off: b.dma_start(
                                  out=wqp_s[:, off:off + HID], in_=d_wq[fc]),
                              waits=[last.get(("wqp", par))], inc="xc%d" % par, amt=16)
                    t_l = P.tok("xc%d" % par)
                    tw = P.op("gpsimd",
                              lambda b, fc=fc, off=off: b.dma_start(
                                  out=d_wqbf[fc], in_=wqp_s[:, off:off + HID]),
                              waits=[t_l], inc="spr", amt=16)
                    wq_wb[fc] = P.tok("spr")
                    if stage == "prep3a":
                        tm = P.op("tensor", lambda b: b.nop(),
                                  waits=[t_l, t_onesb, last.get("wsqev")], inc="mm")
                        tv = tm
                    else:
                        def fm(b, off=off):
                            for kc in range(NKC):
                                i = b.matmul(wsq_ps[0:1, :], onesb_s[:, 0:1],
                                             wqp_s[:, off + kc * 128: off + (kc + 1) * 128],
                                             start=(kc == 0), stop=(kc == NKC - 1))
                            return i
                        tm = P.op("tensor", fm,
                                  waits=[t_l, t_onesb, last.get("wsqev")], inc="mm")
                        tv = P.op("vector",
                                  lambda b, fc=fc: b.tensor_copy(
                                      wsq_s[0:1, fc * 128:(fc + 1) * 128], wsq_ps[0:1, :]),
                                  waits=[tm], inc="vv")
                    last["wsqev"] = tv
                    last[("wqp", par)] = tm
                t_wsq = P.tok("vv")

                for hc in (range(8) if stage not in ("prep0", "prep1", "prep2", "prep3", "prep3a", "prep3b") else []):
                    par = hc % 2
                    off = par * HID
                    def fload(b, hc=hc):
                        r = []
                        for g in range(4):
                            ig = b.dma_start(
                                out=wof_s[:, g * HID:(g + 1) * HID],
                                in_=d_wot[(4 * hc + g) * 128:(4 * hc + g + 1) * 128, :])
                            r.append((ig, 16))
                        return r
                    tl = P.op("sync", fload, waits=[last.get("wofcast")],
                              inc="lpr", amt=64)
                    t_l = P.tok("lpr")
                    if stage == "prep4a":
                        last["wofadd"] = t_l
                        continue
                    def fadd(b):
                        b.tensor_add(wof_s[:, 0:HID], wof_s[:, 0:HID],
                                     wof_s[:, HID:2 * HID])
                        b.tensor_add(wof_s[:, 2 * HID:3 * HID], wof_s[:, 2 * HID:3 * HID],
                                     wof_s[:, 3 * HID:4 * HID])
                        return b.tensor_add(wof_s[:, 0:HID], wof_s[:, 0:HID],
                                            wof_s[:, 2 * HID:3 * HID])
                    tadd = P.op("vector", fadd, waits=[t_l], inc="vv")
                    last["wofadd"] = tadd
                    tc = P.op("scalar",
                              lambda b, off=off: b.activation(
                                  wqp_s[:, off:off + HID], wof_s[:, 0:HID],
                                  AF.Copy),
                              waits=[tadd, last.get(("wqps", par)),
                                     wq_wb.get(6 + par), last.get(("wqp", par))],
                              inc="ss")
                    last["wofcast"] = tc
                    if stage == "prep4b":
                        continue
                    def fstore(b, hc=hc, off=off):
                        src = wqp_s[:, off:off + HID].rearrange("p (oc o) -> p oc o", oc=32)
                        dst = d_wobf[:, :, hc * 128:(hc + 1) * 128].transpose([1, 0, 2])
                        return b.dma_start(out=dst, in_=src)
                    ts = P.op("gpsimd", fstore, waits=[tc], inc="spr", amt=16)
                    last[("wqps", par)] = P.tok("spr")
                t_prep = [P.tok("spr"), P.tok("smis"), t_lpr, t_wsq, P.tok("ss")]
                P.flush(blk, sems)

            NT_EFF = 0 if stage.startswith("prep") else (1 if stage == "a1" else NT)
            xt_s = sbA("xt_s", [128, HID], F32)
            xbf = sbA("xbf", [128, NKC * TT], BF16)
            sig_s = sbA("sig_s", [128, 8 * TT], BF16)
            hpp_s = sbA("hpp_s", [128, 4 * TT], BF16)
            wq_s = sbA("wq_s", [128, 4 * 2048], BF16)
            wo_s = sbA("wo_s", [128, 2 * 2048], BF16)
            psA = lambda n, shp: sA.enter_context(nc.psum_tensor(n, shp, F32))
            tp0 = psA("tp0", [128, TT])
            tp1 = psA("tp1", [128, TT])
            mm0 = psA("mm0", [128, TT])
            mm1 = psA("mm1", [128, TT])
            statE = psA("statE", [1, TT])
            statF = psA("statF", [1, TT])
            tok0G = psA("tok0G", [1, TT])
            bcH = psA("bcH", [128, TT])
            t_l0 = None
            for t in range(NT_EFF):
                ev_tok = {}
                for tc in range(4):
                    wl = [last.get("xtfree")]
                    if t == 0 and tc == 0:
                        wl += t_prep
                    tl = P.op("sync",
                              lambda b, t=t, tc=tc: b.dma_start(
                                  out=xt_s[:, :],
                                  in_=d_x[t * TT + tc * 128: t * TT + (tc + 1) * 128, :]),
                              waits=wl, inc="xin", amt=16)
                    t_l = P.tok("xin")
                    for kg in range(8):
                        pp = tp0 if kg % 2 == 0 else tp1
                        def ftp(b, pp=pp, kg=kg):
                            for j in range(4):
                                i = b.transpose(pp[:, j * 128:(j + 1) * 128],
                                                xt_s[:, (kg * 4 + j) * 128:(kg * 4 + j + 1) * 128],
                                                ident_s[:, :])
                            return i
                        tmm = P.op("tensor", ftp,
                                   waits=[t_l, last.get(("tp", kg % 2)),
                                          (t_lpr if t == 0 and tc == 0 else None)] +
                                         [last.get(("xbfrd", kg * 4 + j)) for j in range(4)],
                                   inc="mm")
                        eng = "vector" if kg % 2 == 0 else "scalar"
                        dst = xbf[:, :].rearrange("p (kc v) -> p kc v", kc=NKC)[
                            :, kg * 4:(kg + 1) * 4, tc * 128:(tc + 1) * 128]
                        emit = (lambda b, pp=pp, dst=dst: b.tensor_copy(
                                    dst, pp[:, :].rearrange("p (j v) -> p j v", j=4))) \
                               if eng == "vector" else \
                               (lambda b, pp=pp, dst=dst: b.activation(
                                    dst, pp[:, :].rearrange("p (j v) -> p j v", j=4), AF.Copy))
                        tev = P.op(eng, emit, waits=[tmm],
                                   inc="vv" if eng == "vector" else "ss")
                        last[("tp", kg % 2)] = tev
                        for j in range(4):
                            ev_tok[(kg * 4 + j, tc)] = tev
                    last["xtfree"] = P.tok("mm")
                smm_last = None
                for kc in range(NKC):
                    par = kc % 2
                    wsq = [ev_tok[(kc, tc)] for tc in range(4)] + [last.get(("sq", par))]
                    ts1 = P.op("scalar",
                               lambda b, kc=kc, par=par: b.activation(
                                   sqpp[:, par * TT:(par + 1) * TT],
                                   xbf[:, kc * TT:(kc + 1) * TT], AF.Square),
                               waits=wsq, inc="ss")
                    wmm = [ts1, t_onesb, (last.get("rowsA_rd") if kc == 0 else None)]
                    tm1 = P.op("tensor",
                               lambda b, kc=kc: b.matmul(
                                   statE[0:1, :], onesb_s[:, 0:1],
                                   xbf[:, kc * TT:(kc + 1) * TT],
                                   start=(kc == 0), stop=(kc == NKC - 1)),
                               waits=wmm, inc="mm")
                    tm2 = P.op("tensor",
                               lambda b, par=par, kc=kc: b.matmul(
                                   statF[0:1, :], onesb_s[:, 0:1],
                                   sqpp[:, par * TT:(par + 1) * TT],
                                   start=(kc == 0), stop=(kc == NKC - 1)),
                               waits=[], inc="mm")
                    last[("sq", par)] = tm2
                    smm_last = tm2

                def f(b):
                    b.tensor_scalar_mul(row(R_M0), statE[0:1, :], 1.0 / HID)
                    b.tensor_scalar_mul(row(R_TMP), statF[0:1, :], 1.0 / HID)
                    b.tensor_tensor(row(R_TMP2), row(R_M0), row(R_M0), ALU.mult)
                    b.tensor_sub(row(R_TMP), row(R_TMP), row(R_TMP2))
                    b.tensor_scalar_add(row(R_TMP), row(R_TMP), EPS_LN)
                    b.reciprocal(row(R_TMP), row(R_TMP))
                    return b.tensor_scalar_mul(row(R_NEGM0), row(R_M0), -1.0)
                t_r0a = P.op("vector", f, waits=[smm_last], inc="vv")
                t_r0b = P.op("scalar",
                             lambda b: b.activation(row(R_RSTD0), row(R_TMP), AF.Sqrt),
                             waits=[t_r0a], inc="ss")
                t_r0c = P.op("vector",
                             lambda b: b.reciprocal(row(R_STD0), row(R_RSTD0)),
                             waits=[t_r0b], inc="vv")
                last["rowsA_rd"] = t_r0a
                t_bc_rstd0 = bcast(row(R_RSTD0), bc1[:, :], [t_r0b])

                sig_tok = {}
                for fc in range(8):
                    pr = fc % 2
                    sem = "wq%d" % pr
                    P.op("sync" if fc % 4 < 2 else "gpsimd",
                         lambda b, fc=fc, pr=pr: b.dma_start(
                             out=wq_s[:, pr * 4096:(pr + 1) * 4096],
                             in_=d_wqbf[fc]),
                         waits=[last.get(("wqpair", pr)),
                                (wq_wb[fc] if t == 0 else None)],
                         inc=sem, amt=16)
                    tl = P.tok(sem)
                    pq = mm0 if fc % 2 == 0 else mm1
                    wmm = [tl, t_r0a, t_r0c, last.get(("mmq", fc % 2))]
                    wmm += [ev_tok[(kc, tc)] for kc in range(NKC) for tc in range(4)] \
                        if fc == 0 else []
                    def fq(b, fc=fc, pq=pq, pr=pr):
                        for kc in range(NKC):
                            b.matmul(pq[:, :],
                                     wq_s[:, pr * 4096 + kc * 128: pr * 4096 + (kc + 1) * 128],
                                     xbf[:, kc * TT:(kc + 1) * TT],
                                     start=(kc == 0), stop=False)
                        b.matmul(pq[:, :], wsq_s[0:1, fc * 128:(fc + 1) * 128],
                                 row(R_NEGM0), start=False, stop=False)
                        return b.matmul(pq[:, :], bqr_s[0:1, fc * 128:(fc + 1) * 128],
                                        row(R_STD0), start=False, stop=True)
                    tq = P.op("tensor", fq, waits=wmm, inc="mm")
                    last[("wqpair", pr)] = tq
                    par = fc % 2
                    t1 = P.op("vector",
                              lambda b, pq=pq, par=par: b.tensor_mul(
                                  tmpa[:, par * TT:(par + 1) * TT], pq[:, :], bc1[:, :]),
                              waits=[tq, t_bc_rstd0, last.get(("tmpaA", par))], inc="vv")
                    t2 = P.op("vector",
                              lambda b, par=par: b.tensor_scalar_min(
                                  tmpb[:, par * TT:(par + 1) * TT],
                                  tmpa[:, par * TT:(par + 1) * TT], 0.0),
                              waits=[t1], inc="vv")
                    t3 = P.op("scalar",
                              lambda b, par=par: b.activation(
                                  tmpb[:, par * TT:(par + 1) * TT],
                                  tmpb[:, par * TT:(par + 1) * TT], AF.Exp),
                              waits=[t2], inc="ss")
                    t4 = P.op("vector",
                              lambda b, fc=fc, par=par: b.scalar_tensor_tensor(
                                  sig_s[:, fc * TT:(fc + 1) * TT],
                                  tmpa[:, par * TT:(par + 1) * TT], 0.0,
                                  tmpb[:, par * TT:(par + 1) * TT], ALU.max, ALU.add),
                              waits=[t3], inc="vv")
                    last[("mmq", fc % 2)] = t4
                    last[("tmpaA", par)] = t4
                    sig_tok[fc] = t4

                retr_tok = {}
                for h in range(8):
                    pq = mm0 if h % 2 == 0 else mm1
                    tnum = P.op("tensor",
                                lambda b, h=h, pq=pq: b.matmul(
                                    pq[:, :], mz_s[:, h * 128:(h + 1) * 128],
                                    sig_s[:, h * TT:(h + 1) * TT],
                                    start=True, stop=True),
                                waits=[sig_tok[h], t_casts, last.get(("mmq", h % 2))],
                                inc="mm")
                    tden = P.op("tensor",
                                lambda b, h=h: b.matmul(
                                    statF[0:1, :], mz_s[:, 8 * 128 + h:8 * 128 + h + 1],
                                    sig_s[:, h * TT:(h + 1) * TT],
                                    start=True, stop=True),
                                waits=[last.get("denF"), (t_r0a if h == 0 else None)],
                                inc="mm")
                    td1 = P.op("vector",
                               lambda b: b.tensor_scalar_add(
                                   row(R_DEN), statF[0:1, :], EPS_ATTN),
                               waits=[tden, last.get("denrow")], inc="vv")
                    last["denF"] = td1
                    td2 = P.op("vector",
                               lambda b: b.reciprocal(row(R_DEN), row(R_DEN)),
                               waits=[td1], inc="vv")
                    tb = bcast(row(R_DEN), bc2[:, :], [td2])
                    last["denrow"] = tb
                    tev = P.op("vector",
                               lambda b, h=h, pq=pq: b.tensor_mul(
                                   sig_s[:, h * TT:(h + 1) * TT], pq[:, :], bc2[:, :]),
                               waits=[tnum, tb], inc="vv")
                    last[("mmq", h % 2)] = tev
                    retr_tok[h] = tev
                retr_all = [retr_tok[h] for h in range(8)]

                at_tok = {}
                amm_last = None
                for og in range(16):
                    par = og % 2
                    sem = "wo%d" % par
                    tl = P.op("sync",
                              lambda b, og=og, par=par: b.dma_start(
                                  out=wo_s[:, par * 2048:(par + 1) * 2048]
                                  .rearrange("p (o v) -> p o v", o=2),
                                  in_=d_wobf.transpose([1, 0, 2])[:, og * 2:og * 2 + 2, :]),
                              waits=[last.get(("wos", par)),
                                     (t_prep[0] if t == 0 and og < 2 else None)],
                              inc=sem, amt=16)
                    t_l = P.tok(sem)
                    for j in range(2):
                        oc = og * 2 + j
                        pa = mm0 if oc % 2 == 0 else mm1
                        wmm = [t_l, last.get(("mmq", oc % 2))] + \
                              (retr_all if oc == 0 else [])
                        def fa(b, pa=pa, par=par, j=j):
                            for hc in range(8):
                                i = b.matmul(pa[:, :],
                                             wo_s[:, par * 2048 + j * 1024 + hc * 128:
                                                  par * 2048 + j * 1024 + (hc + 1) * 128],
                                             sig_s[:, hc * TT:(hc + 1) * TT],
                                             start=(hc == 0), stop=(hc == 7))
                            return i
                        ta = P.op("tensor", fa, waits=wmm, inc="mm")
                        last[("wos", par)] = ta
                        te1 = P.op("scalar",
                                   lambda b, pa=pa, oc=oc: b.activation(
                                       bigbf[:, oc * TT:(oc + 1) * TT], pa[:, :],
                                       AF.Copy),
                                   waits=[ta, last.get(("bigw", oc))], inc="ss")
                        spar = oc % 2
                        te2 = P.op("scalar",
                                   lambda b, pa=pa, spar=spar: b.activation(
                                       sqpp[:, spar * TT:(spar + 1) * TT], pa[:, :],
                                       AF.Square),
                                   waits=[ta, last.get(("sq", spar))], inc="ss")
                        last[("mmq", oc % 2)] = te2
                        wm1 = [te1, te2] + ([t_r0a, last.get("rowsA_rd")] if oc == 0 else [])
                        tm1 = P.op("tensor",
                                   lambda b, oc=oc: b.matmul(
                                       statE[0:1, :], onesb_s[:, 0:1],
                                       bigbf[:, oc * TT:(oc + 1) * TT],
                                       start=(oc == 0), stop=(oc == NKC - 1)),
                                   waits=wm1, inc="mm")
                        tm2 = P.op("tensor",
                                   lambda b, spar=spar, oc=oc: b.matmul(
                                       statF[0:1, :], onesb_s[:, 0:1],
                                       sqpp[:, spar * TT:(spar + 1) * TT],
                                       start=(oc == 0), stop=(oc == NKC - 1)),
                                   waits=([last.get("denF")] if oc == 0 else []),
                                   inc="mm")
                        last[("sq", spar)] = tm2
                        amm_last = tm2
                        at_tok[oc] = te1

                def f(b):
                    b.tensor_scalar_mul(row(R_MA), statE[0:1, :], 1.0 / HID)
                    b.tensor_scalar_mul(row(R_TMP), statF[0:1, :], 1.0 / HID)
                    b.tensor_tensor(row(R_TMP2), row(R_MA), row(R_MA), ALU.mult)
                    b.tensor_sub(row(R_TMP), row(R_TMP), row(R_TMP2))
                    b.tensor_scalar_add(row(R_TMP), row(R_TMP), EPS_LN)
                    b.reciprocal(row(R_TMP), row(R_TMP))
                    return b.tensor_scalar_mul(row(R_NEGMA), row(R_MA), -1.0)
                t_r1a = P.op("vector", f, waits=[amm_last], inc="vv")
                t_r1b = P.op("scalar",
                             lambda b: b.activation(row(R_RSTDA), row(R_TMP), AF.Sqrt),
                             waits=[t_r1a], inc="ss")
                last["rowsA_rd"] = t_r1a
                t_bc_negm1 = bcast(row(R_NEGMA), bc2[:, :], [t_r1a, retr_all[-1]])
                t_bc_rstd1 = bcast(row(R_RSTDA), bc3[:, :], [t_r1b])

                x1st_tok = {}
                x1sq_last = None
                for oc in range(NKC):
                    par = oc % 2
                    slot = oc % 2
                    wts = [t_bc_negm1, t_bc_rstd1, at_tok[oc],
                           last.get(("tmpaA", par)), last.get(("x1slot", slot))]
                    t1 = P.op("vector",
                              lambda b, oc=oc, par=par: b.tensor_add(
                                  tmpa[:, par * TT:(par + 1) * TT],
                                  bigbf[:, oc * TT:(oc + 1) * TT], bc2[:, :]),
                              waits=wts, inc="vv")
                    t2 = P.op("vector",
                              lambda b, par=par: b.tensor_mul(
                                  tmpa[:, par * TT:(par + 1) * TT],
                                  tmpa[:, par * TT:(par + 1) * TT], bc3[:, :]),
                              waits=[t1], inc="vv")
                    t3 = P.op("vector",
                              lambda b, oc=oc, par=par, slot=slot: b.tensor_add(
                                  x1c_s[:, slot * TT:(slot + 1) * TT],
                                  tmpa[:, par * TT:(par + 1) * TT],
                                  xbf[:, oc * TT:(oc + 1) * TT]),
                              waits=[t2], inc="vv")
                    last[("tmpaA", par)] = t3
                    last[("bigw", oc)] = t3
                    last[("xbfrd", oc)] = t3
                    ts1 = P.op("scalar",
                               lambda b, par=par, slot=slot: b.activation(
                                   sqpp[:, par * TT:(par + 1) * TT],
                                   x1c_s[:, slot * TT:(slot + 1) * TT], AF.Square),
                               waits=[t3, last.get(("sq", par))], inc="ss")
                    tm = P.op("tensor",
                              lambda b, oc=oc, par=par: b.matmul(
                                  statE[0:1, :], onesb_s[:, 0:1],
                                  sqpp[:, par * TT:(par + 1) * TT],
                                  start=(oc == 0), stop=(oc == NKC - 1)),
                              waits=[ts1] + ([t_r1a, last.get("rowsA_rd")] if oc == 0 else []),
                              inc="mm")
                    last[("sq", par)] = tm
                    x1sq_last = tm
                    if slot == 1:
                        g = oc // 2
                        tg = P.op("scalar",
                                  lambda b, t=t, g=g: b.dma_start(
                                      out=d_x1s[t][:, g * 2 * TT:(g + 1) * 2 * TT],
                                      in_=x1c_s[:, :]),
                                  waits=[last[("tmpaA", 0)], last[("tmpaA", 1)]],
                                  inc="sx1", amt=16)
                        t_g = P.tok("sx1")
                        for s_ in range(2):
                            last[("x1slot", s_)] = t_g
                        x1st_tok[g] = t_g

                def f(b):
                    b.tensor_scalar_mul(row(R_TMP), statE[0:1, :], 1.0 / HID)
                    b.tensor_tensor(row(R_TMP2), row(R_M0), row(R_M0), ALU.mult)
                    b.tensor_sub(row(R_TMP), row(R_TMP), row(R_TMP2))
                    b.tensor_scalar_add(row(R_TMP), row(R_TMP), EPS_LN)
                    return b.reciprocal(row(R_TMP), row(R_TMP))
                t_r2a = P.op("vector", f, waits=[x1sq_last], inc="vv")
                t_r2b = P.op("scalar",
                             lambda b: b.activation(row(R_RSTD2), row(R_TMP), AF.Sqrt),
                             waits=[t_r2a], inc="ss")
                last["rowsA_rd"] = t_r2a
                t_bc_negm0 = bcast(row(R_NEGM0), bc0[:, :], [t_r2a])
                t_bc_rstd2 = bcast(row(R_RSTD2), bc3[:, :],
                                   [t_r2b, last[("tmpaA", 0)], last[("tmpaA", 1)]])

                if t == 0:
                    def f(b):
                        b.tensor_copy(cell(19), rows[0:1, R_NEGM0:R_NEGM0 + 1])
                        return b.tensor_copy(cell(20), rows[0:1, R_RSTD2:R_RSTD2 + 1])
                    t_c0 = P.op("vector", f, waits=[t_r2a, t_r2b, t_bc_rstd2], inc="vv")
                    tmc = P.op("tensor",
                               lambda b: b.matmul(
                                   bcH[:, 0:2], onesr_s[0:1, 0:128],
                                   rows[0:1, R_CELL + 19:R_CELL + 21],
                                   start=True, stop=True),
                               waits=[t_c0, last.get("bcfree")], inc="mm")
                    t_cols = P.op("scalar",
                                  lambda b: b.activation(colw[:, 0:2], bcH[:, 0:2], AF.Copy),
                                  waits=[tmc], inc="ss")
                    last["bcfree"] = t_cols

                habs_last = None
                tg_prev = None
                for oc in range(NKC):
                    par = oc % 2
                    slot = oc % 4
                    sem = "xc%d" % par
                    tl = P.op("scalar",
                              lambda b, t=t, oc=oc, par=par: b.dma_start(
                                  out=tmpb[:, par * TT:(par + 1) * TT],
                                  in_=d_x1s[t][:, oc * TT:(oc + 1) * TT]),
                              waits=[last.get(("tmpbA", par))] +
                                    ([P.tok("sx1")] if oc == 0 else []),
                              inc=sem, amt=16)
                    t_l = P.tok(sem)
                    t1 = P.op("vector",
                              lambda b, par=par: b.tensor_add(
                                  tmpa[:, par * TT:(par + 1) * TT],
                                  tmpb[:, par * TT:(par + 1) * TT], bc0[:, :]),
                              waits=[t_l, t_bc_negm0, last.get(("tmpaA", par))],
                              inc="vv")
                    t2 = P.op("vector",
                              lambda b, par=par, slot=slot: b.tensor_mul(
                                  hpp_s[:, slot * TT:(slot + 1) * TT],
                                  tmpa[:, par * TT:(par + 1) * TT], bc3[:, :]),
                              waits=[t1, t_bc_rstd2, last.get(("hslot", slot))],
                              inc="vv")
                    last[("tmpaA", par)] = t2
                    ts1 = P.op("scalar",
                               lambda b, par=par, slot=slot: b.activation(
                                   sqpp[:, par * TT:(par + 1) * TT],
                                   hpp_s[:, slot * TT:(slot + 1) * TT], AF.Abs),
                               waits=[t2, last.get(("sq", par))], inc="ss")
                    tm = P.op("tensor",
                              lambda b, par=par, oc=oc: b.matmul(
                                  statF[0:1, :], onesb_s[:, 0:1],
                                  sqpp[:, par * TT:(par + 1) * TT],
                                  start=(oc == 0), stop=(oc == NKC - 1)),
                              waits=[ts1] + ([t_r1a, t_r2a] if oc == 0 else []),
                              inc="mm")
                    last[("sq", par)] = tm
                    habs_last = tm
                    if t == 0:
                        tc1 = P.op("vector",
                                   lambda b, par=par: b.tensor_add(
                                       colw[:, 2:3], tmpb[:, par * TT:par * TT + 1],
                                       colw[:, 0:1]),
                                   waits=[t_l, t_cols, (tg_prev or None)], inc="vv")
                        tc2 = P.op("vector",
                                   lambda b: b.tensor_mul(colw[:, 3:4], colw[:, 2:3],
                                                          colw[:, 1:2]),
                                   waits=[tc1], inc="vv")
                        tg_prev = P.op("tensor",
                                       lambda b, oc=oc: b.matmul(
                                           tok0G[0:1, 0:3], colw[:, 3:4],
                                           gwm_s[:, oc * 3:(oc + 1) * 3],
                                           start=(oc == 0), stop=(oc == NKC - 1)),
                                       waits=[tc2, t_lpr],
                                       inc="mm")
                    last[("tmpbA", par)] = (tm if t != 0 else P.tok("vv"))
                    if slot == 3:
                        g = oc // 4
                        tg = P.op("scalar",
                                  lambda b, t=t, g=g: b.dma_start(
                                      out=d_hs[t][:, g * 4 * TT:(g + 1) * 4 * TT],
                                      in_=hpp_s[:, :]),
                                  waits=[last[("tmpaA", 0)], last[("tmpaA", 1)]],
                                  inc="sh", amt=16)
                        t_g = P.tok("sh")
                        for s_ in range(4):
                            last[("hslot", s_)] = t_g
                if t == 0:
                    t_l0 = P.op("vector",
                                lambda b: b.tensor_copy(
                                    rows[0:1, R_CELL + 2:R_CELL + 5], tok0G[0:1, 0:3]),
                                waits=[tg_prev], inc="vv")

                def f(b, t=t):
                    b.tensor_copy(row(R_SABS), statF[0:1, :])
                    b.tensor_mul(row(R_TMP), row(R_SABS),
                                 maskv_s[0:1, t * TT:(t + 1) * TT])
                    b.tensor_reduce(cell(15), row(R_TMP), AX.X, ALU.add)
                    return b.tensor_reduce(cell(16), row(R_SABS), AX.X, ALU.add)
                t_sega = P.op("vector", f, waits=[habs_last], inc="vv")
                def f(b, t=t):
                    if t == 0:
                        b.tensor_copy(cell(0), cell(15))
                        return b.tensor_copy(cell(1), cell(16))
                    b.tensor_add(cell(25), cell(0), cell(15))
                    return b.tensor_add(cell(26), cell(1), cell(16))
                t_segb = P.op("vector", f, waits=[t_sega], inc="vv")
                if t == 0:
                    t_seg = t_segb
                else:
                    def f(b):
                        b.tensor_copy(cell(0), cell(25))
                        return b.tensor_copy(cell(1), cell(26))
                    t_seg = P.op("vector", f, waits=[t_segb], inc="vv")
                last["rowsA_rd"] = t_sega

            do_ar = stage in ("ar", "all") and NT_EFF == NT
            def f1(b):
                return b.tensor_sub(cell(1), cell(1), cell(0))
            def f2(b):
                b.tensor_scalar(arst_s[0:1, 0:1], cell(0), meta_s[0:1, 0:1], None, ALU.mult)
                b.tensor_scalar(arst_s[0:1, 1:2], cell(1), meta_s[0:1, 0:1], None, ALU.mult)
                b.tensor_scalar(arst_s[0:1, 2:3], cell(0), meta_s[0:1, 1:2], None, ALU.mult)
                b.tensor_scalar(arst_s[0:1, 3:4], cell(1), meta_s[0:1, 1:2], None, ALU.mult)
                return b.tensor_scalar(rows[0:1, R_CELL + 2:R_CELL + 5],
                                       rows[0:1, R_CELL + 2:R_CELL + 5],
                                       meta_s[0:1, 2:3], None, ALU.mult)
            def f3(b):
                b.tensor_scalar(arst_s[0:1, 4:7], rows[0:1, R_CELL + 2:R_CELL + 5],
                                meta_s[0:1, 0:1], None, ALU.mult)
                b.tensor_scalar(arst_s[0:1, 7:10], rows[0:1, R_CELL + 2:R_CELL + 5],
                                meta_s[0:1, 1:2], None, ALU.mult)
                return b.memset(arst_s[0:1, 10:32], 0.0)
            if do_ar:
                ta1 = P.op("vector", f1, waits=[t_seg, t_l0, t_lpr], inc="vv")
                ta2 = P.op("vector", f2, waits=[ta1], inc="vv")
                t_arst = P.op("vector", f3, waits=[ta2], inc="vv")
                P.op("gpsimd", lambda b: b.dma_start(out=d_ccin, in_=arst_s[0:1, :]),
                     waits=[t_arst], inc="smis", amt=16)
                t_cc = P.op("gpsimd",
                            lambda b: b.collective_compute(
                                "AllReduce", ALU.add,
                                replica_groups=[[0, 1, 2, 3, 4, 5, 6, 7]],
                                ins=[d_ccin], outs=[d_ccout]),
                            waits=[P.tok("smis")], inc="gg")
                P.op("gpsimd", lambda b: b.dma_start(out=arout_s[0:1, :], in_=d_ccout),
                     waits=[t_cc], inc="smis", amt=16)
            t_ar = P.tok("smis")

            if DEBUG_TAPS:
                def ftap2(b):
                    r = []
                    r.append((b.dma_start(out=d_rowdbg, in_=rows[0:1, :]), 16))
                    r.append((b.dma_start(out=d_atdbg, in_=bigbf[:, :]), 16))
                    r.append((b.dma_start(out=d_rtdbg, in_=sig_s[:, :]), 16))
                    return r
                P.op("gpsimd", ftap2,
                     waits=[P.tok("vv"), P.tok("ss"), P.tok("mm")], inc="smis", amt=48)
                def ftap(b):
                    r = []
                    for t_ in range(NT):
                        r.append((b.dma_start(out=d_x1dbg[t_], in_=d_x1s[t_]), 16))
                        r.append((b.dma_start(out=d_hdbg[t_], in_=d_hs[t_]), 16))
                    return r
                P.op("gpsimd", ftap,
                     waits=[P.tok("sx1"), P.tok("sh")], inc="smis", amt=16 * 2 * NT)
                P.op("gpsimd", lambda b: b.dma_start(out=d_ardbg, in_=arout_s[0:1, :]),
                     waits=[t_ar], inc="smis", amt=16)
            joinA = [P.tok(s) for s in ("mm", "vv", "ss", "gg", "sx1", "sh",
                                        "smis", "spr", "xin", "wq0", "wq1",
                                        "wo0", "wo1", "xc0", "xc1")]
            P.flush(blk, sems)

        if stage != "all":
            P.flush(blk, sems)
            return nc
        with ExitStack() as sC:
            sbC = lambda n, shp, dt: sC.enter_context(nc.sbuf_tensor(n, shp, dt))
            act_s = sbC("act_s", [128, NKC * TT], BF16)
            moeacc = sbC("moeacc", [128, NKC * TT], F32)
            ws_s = sbC("ws_s", [128, 4 * 2048], BF16)
            x2t_s = sbC("x2t_s", [128, TT], F32)
            coefb = sbC("coefb", [128, 3 * TT], BF16)
            coef3 = sbC("coef3", [3, TT], BF16)

            psC = lambda n, shp: sC.enter_context(nc.psum_tensor(n, shp, F32))
            pre0 = psC("pre0", [128, TT])
            pre1 = psC("pre1", [128, TT])
            out0 = psC("out0", [128, TT])
            out1 = psC("out1", [128, TT])
            rowsC = psC("rowsC", [3, TT])
            statF2 = psC("statF2", [1, TT])
            bcH2 = psC("bcH2", [128, TT])
            tpo0 = psC("tpo0", [128, TT])
            tpo1 = bcH2

            last = {"bcfree": None}

            def bcast2(src_row, dst_tile, waits, dst2=None):
                tm = P.op("tensor",
                          lambda b, src_row=src_row: b.matmul(
                              bcH2[:, :], onesr_s[0:1, 0:128], src_row,
                              start=True, stop=True),
                          waits=list(waits) + [last.get("bcfree"),
                                               last.get(("tpoev", 1))],
                          inc="mm")
                te = P.op("scalar",
                          lambda b, dst_tile=dst_tile: b.activation(
                              dst_tile, bcH2[:, :], AF.Copy),
                          waits=[tm], inc="ss")
                last["bcfree"] = te
                return te

            def g1(b):
                b.tensor_scalar_mul(cell(17), arout_s[0:1, 0:1], 1.0 / (VIS_LEN * HID))
                b.tensor_scalar_mul(cell(18), arout_s[0:1, 1:2],
                                    1.0 / ((S_FULL - VIS_LEN) * HID))
                b.tensor_scalar(cell(15), arout_s[0:1, 2:3], meta_s[0:1, 1:2], None, ALU.mult)
                return b.tensor_scalar(cell(16), arout_s[0:1, 3:4], meta_s[0:1, 1:2],
                                       None, ALU.mult)
            tg1 = P.op("vector", g1, waits=joinA + [t_ar], inc="vv")
            def g2(b):
                b.tensor_scalar(cell(15), cell(15), 1.0 / (VIS_LEN * HID), None, ALU.mult)
                b.tensor_scalar(cell(16), cell(16), 1.0 / ((S_FULL - VIS_LEN) * HID),
                                None, ALU.mult)
                b.tensor_scalar(cell(19), cell(17), meta_s[0:1, 0:1], None, ALU.mult)
                return b.tensor_scalar(cell(20), cell(18), meta_s[0:1, 0:1], None, ALU.mult)
            tg2 = P.op("vector", g2, waits=[tg1], inc="vv")
            def g3(b):
                b.tensor_add(cell(17), cell(19), cell(15))
                return b.tensor_add(cell(18), cell(20), cell(16))
            tg3 = P.op("vector", g3, waits=[tg2], inc="vv")
            tg4 = P.op("vector", lambda b: b.tensor_scalar_add(cell(21), cell(17), 1e-8),
                       waits=[tg3], inc="vv")
            tg5 = P.op("vector", lambda b: b.reciprocal(cell(22), cell(21)),
                       waits=[tg4], inc="vv")
            tg6 = P.op("vector", lambda b: b.tensor_mul(cell(5), cell(18), cell(22)),
                       waits=[tg5], inc="vv")
            def g7(b):
                b.tensor_scalar(rows[0:1, R_CELL + 2:R_CELL + 5], arout_s[0:1, 4:7],
                                meta_s[0:1, 0:1], None, ALU.mult)
                b.tensor_scalar(rows[0:1, R_CELL + 28:R_CELL + 31], arout_s[0:1, 7:10],
                                meta_s[0:1, 1:2], None, ALU.mult)
                return b.tensor_scalar(rows[0:1, R_CELL + 31:R_CELL + 34], gwl_s[0:1, :],
                                       cell(5), None, ALU.mult)
            tg7 = P.op("vector", g7, waits=[tg6], inc="vv")
            def g8(b):
                b.tensor_add(rows[0:1, R_CELL + 2:R_CELL + 5],
                             rows[0:1, R_CELL + 2:R_CELL + 5],
                             rows[0:1, R_CELL + 28:R_CELL + 31])
                return b.tensor_add(rows[0:1, R_CELL + 6:R_CELL + 9],
                                    rows[0:1, R_CELL + 31:R_CELL + 34], gbr_s[0:1, :])
            tg8 = P.op("vector", g8, waits=[tg7], inc="vv")
            tg9 = P.op("vector",
                       lambda b: b.tensor_add(rows[0:1, R_CELL + 2:R_CELL + 5],
                                              rows[0:1, R_CELL + 2:R_CELL + 5],
                                              rows[0:1, R_CELL + 31:R_CELL + 34]),
                       waits=[tg8], inc="vv")
            t_gate0 = P.op("vector",
                           lambda b: b.tensor_add(rows[0:1, R_CELL + 2:R_CELL + 5],
                                                  rows[0:1, R_CELL + 2:R_CELL + 5],
                                                  gbr_s[0:1, :]),
                           waits=[tg9], inc="vv")

            def top_ind(out_off, waits):
                def fa(b):
                    L = lambda i: rows[0:1, R_CELL + 2 + i:R_CELL + 3 + i]
                    b.tensor_tensor(cell(15), L(0), L(1), ALU.is_ge)
                    b.tensor_tensor(cell(16), L(0), L(2), ALU.is_ge)
                    return b.tensor_tensor(cell(21), L(1), L(2), ALU.is_ge)
                t1 = P.op("vector", fa, waits=waits, inc="vv")
                def fb(b):
                    O = lambda i: rows[0:1, out_off + i:out_off + i + 1]
                    b.tensor_mul(O(0), cell(15), cell(16))
                    b.tensor_scalar(cell(23), cell(15), -1.0, 1.0, ALU.mult, ALU.add)
                    b.tensor_scalar(cell(24), cell(16), -1.0, 1.0, ALU.mult, ALU.add)
                    return b.tensor_scalar(cell(27), cell(21), -1.0, 1.0, ALU.mult, ALU.add)
                t2 = P.op("vector", fb, waits=[t1], inc="vv")
                def fc(b):
                    O = lambda i: rows[0:1, out_off + i:out_off + i + 1]
                    b.tensor_mul(O(1), cell(23), cell(21))
                    return b.tensor_mul(O(2), cell(24), cell(27))
                return P.op("vector", fc, waits=[t2], inc="vv")

            t_i1 = top_ind(R_CELL + 9, [t_gate0])
            t_m1 = P.op("vector",
                        lambda b: b.tensor_scalar(rows[0:1, R_CELL + 34:R_CELL + 37],
                                                  rows[0:1, R_CELL + 9:R_CELL + 12],
                                                  -1e30, None, ALU.mult),
                        waits=[t_i1], inc="vv")
            t_msk = P.op("vector",
                         lambda b: b.tensor_add(rows[0:1, R_CELL + 2:R_CELL + 5],
                                                rows[0:1, R_CELL + 2:R_CELL + 5],
                                                rows[0:1, R_CELL + 34:R_CELL + 37]),
                         waits=[t_m1], inc="vv")
            t_i2 = top_ind(R_CELL + 12, [t_msk])

            PHI = {0: AF.Tanh, 1: AF.Gelu, 2: AF.Silu}
            for t in range(NT):
                tlh = P.op("sync",
                           lambda b, t=t: b.dma_start(out=bigbf[:, :], in_=d_hs[t]),
                           waits=[last.get("hfree0"), last.get("hfree1")] +
                                 (joinA if t == 0 else []) +
                                 ([P.tok("sh")] if t == 0 else []),
                           inc="hld", amt=16)
                t_h = P.tok("hld")

                def fg(b):
                    for kc in range(NKC):
                        i = b.matmul(rowsC[0:3, :], gwb_s[:, kc * 3:(kc + 1) * 3],
                                     bigbf[:, kc * TT:(kc + 1) * TT],
                                     start=(kc == 0), stop=(kc == NKC - 1))
                    return i
                t_gmm = P.op("tensor", fg,
                             waits=[t_h, t_casts, last.get("rowsC_rd")], inc="mm")
                t_gev = P.op("vector",
                             lambda b: b.tensor_copy(rows2[0:3, :], rowsC[0:3, :]),
                             waits=[t_gmm, last.get("rows2f")], inc="vv")
                last["rowsC_rd"] = t_gev
                def fmv(b):
                    r = []
                    for e in range(3):
                        r.append((b.dma_start(out=rows[0:1, R_L + e * TT:R_L + (e + 1) * TT],
                                              in_=rows2[e:e + 1, :]), 16))
                    return r
                t_gmv = P.op("gpsimd", fmv, waits=[t_gev, last.get("rowsL_rd")],
                             inc="smis", amt=48)
                t_gmvt = P.tok("smis")

                def fs(b):
                    LG = lambda e: row(R_L + e * TT)
                    b.tensor_scalar(LG(0), LG(0), cell(6), None, ALU.add)
                    b.tensor_scalar(LG(1), LG(1), cell(7), None, ALU.add)
                    b.tensor_scalar(LG(2), LG(2), cell(8), None, ALU.add)
                    b.tensor_max(row(R_TMP), LG(0), LG(1))
                    b.tensor_max(row(R_TMP), row(R_TMP), LG(2))
                    b.tensor_tensor(row(R_TMP2), LG(0), LG(1), ALU.min)
                    b.tensor_tensor(row(R_TMP2), row(R_TMP2), LG(2), ALU.min)
                    b.tensor_sub(row(R_E + 0 * TT), LG(0), row(R_TMP))
                    b.tensor_sub(row(R_E + 1 * TT), LG(1), row(R_TMP))
                    b.tensor_sub(row(R_E + 2 * TT), LG(2), row(R_TMP))
                    return b.tensor_sub(row(R_E + 3 * TT), row(R_TMP2), row(R_TMP))
                t_sm1 = P.op("vector", fs, waits=[t_gmvt, t_gate0, t_i2], inc="vv")
                t_sm2 = P.op("scalar",
                             lambda b: b.activation(rows[0:1, R_E:R_E + 4 * TT],
                                                    rows[0:1, R_E:R_E + 4 * TT], AF.Exp),
                             waits=[t_sm1], inc="ss")
                def fs2(b):
                    E = lambda i: row(R_E + i * TT)
                    b.tensor_add(row(R_S), E(0), E(1))
                    b.tensor_add(row(R_S), row(R_S), E(2))
                    b.reciprocal(row(R_RECS), row(R_S))
                    b.tensor_sub(row(R_A2), row(R_S), E(3))
                    b.tensor_mul(row(R_A2), row(R_A2), row(R_RECS))
                    b.tensor_scalar_add(row(R_A2), row(R_A2), 1e-8)
                    b.reciprocal(row(R_A2), row(R_A2))
                    b.tensor_mul(row(R_WTOP), row(R_RECS), row(R_A2))
                    b.tensor_scalar_add(row(R_W2ND), row(R_S), -1.0)
                    b.tensor_sub(row(R_W2ND), row(R_W2ND), E(3))
                    b.tensor_mul(row(R_W2ND), row(R_W2ND), row(R_RECS))
                    b.tensor_mul(row(R_W2ND), row(R_W2ND), row(R_A2))
                    r = None
                    for e in range(3):
                        b.tensor_scalar(row(R_TMP), row(R_WTOP), cell(9 + e), None, ALU.mult)
                        b.tensor_scalar(row(R_TMP2), row(R_W2ND), cell(12 + e), None, ALU.mult)
                        r = b.tensor_add(row(R_L + e * TT), row(R_TMP), row(R_TMP2))
                    return r
                t_coefr = P.op("vector", fs2, waits=[t_sm2], inc="vv")
                last["rowsL_rd"] = t_coefr
                def fc3(b):
                    r = []
                    for e in range(3):
                        r.append((b.dma_start(out=coef3[e:e + 1, :],
                                              in_=rows[0:1, R_L + e * TT:R_L + (e + 1) * TT]),
                                  16))
                    return r
                t_c3 = P.op("gpsimd", fc3,
                            waits=[t_coefr, last.get("coef3_rd")], inc="smis", amt=48)
                t_c3t = P.tok("smis")
                tcb = {}
                for e in range(3):
                    tcb[e] = bcast2(row(R_L + e * TT), coefb[:, e * TT:(e + 1) * TT],
                                    [t_coefr, last.get(("coefb", e))])

                moe_ev = {}
                mstat_last = None
                PAIR_SEMS = ("wa0", "wa1")
                if ("wu",) not in last:
                    last[("wu",)] = 0
                def wload(engs, dsrc, extra_waits):
                    u = last[("wu",)]
                    last[("wu",)] = u + 1
                    pr = u % 2
                    sem = PAIR_SEMS[pr]
                    eng = engs[u % len(engs)]
                    P.op(eng,
                         lambda b, dsrc=dsrc, pr=pr: b.dma_start(
                             out=ws_s[:, pr * 4096:(pr + 1) * 4096], in_=dsrc),
                         waits=[last.get(("wpair", pr))] + list(extra_waits),
                         inc=sem, amt=16)
                    return P.tok(sem), pr
                for e in range(3):
                    act_tok = {}
                    for fc in range(NKC):
                        ew = [t_conv[e], joinA[0]] if (t == 0 and fc == 0) else []
                        tl, pr = wload(("sync", "gpsimd"), d_w1bf[e, fc], ew)
                        pp = pre0 if fc % 2 == 0 else pre1
                        wmm = [tl, t_h, last.get(("pre", fc % 2))]
                        def fp(b, pp=pp, pr=pr):
                            for k in range(NKC):
                                i = b.matmul(pp[:, :],
                                             ws_s[:, pr * 4096 + k * 128: pr * 4096 + (k + 1) * 128],
                                             bigbf[:, k * TT:(k + 1) * TT],
                                             start=(k == 0), stop=(k == NKC - 1))
                            return i
                        tp = P.op("tensor", fp, waits=wmm, inc="mm")
                        last[("wpair", pr)] = tp
                        ta = P.op("scalar",
                                  lambda b, pp=pp, e=e, fc=fc: b.activation(
                                      act_s[:, fc * TT:(fc + 1) * TT], pp[:, :],
                                      PHI[e], bias=b1c_s[:, e * 32 + fc:e * 32 + fc + 1]),
                                  waits=[tp, t_lpr, last.get(("actw", fc))], inc="ss")
                        last[("pre", fc % 2)] = ta
                        tsc = P.op("vector",
                                   lambda b, e=e, fc=fc: b.tensor_mul(
                                       act_s[:, fc * TT:(fc + 1) * TT],
                                       act_s[:, fc * TT:(fc + 1) * TT],
                                       coefb[:, e * TT:(e + 1) * TT]),
                                   waits=[ta, tcb[e]], inc="vv")
                        act_tok[fc] = tsc
                    act_all = [act_tok[fc] for fc in range(NKC)]
                    last[("coefb", e)] = act_all[-1]
                    for dc in range(NKC):
                        ew = [t_conv[3 + e]] if (t == 0 and dc == 0) else []
                        tl, pr = wload(("scalar", "gpsimd"), d_w2bf[e, dc], ew)
                        po = out0 if dc % 2 == 0 else out1
                        wmm = [tl, last.get(("out", dc % 2))]
                        if dc == 0:
                            wmm += act_all
                        if e == 2:
                            wmm.append(t_c3t)
                        def fo(b, po=po, pr=pr, e=e, dc=dc):
                            for k in range(NKC):
                                lastmm = (k == NKC - 1 and e != 2)
                                i = b.matmul(po[:, :],
                                             ws_s[:, pr * 4096 + k * 128: pr * 4096 + (k + 1) * 128],
                                             act_s[:, k * TT:(k + 1) * TT],
                                             start=(k == 0), stop=lastmm)
                            if e == 2:
                                i = b.matmul(po[:, :], b2s_s[0:3, dc * 128:(dc + 1) * 128],
                                             coef3[0:3, :], start=False, stop=True)
                            return i
                        tmo = P.op("tensor", fo, waits=wmm + [t_coefr], inc="mm")
                        last[("wpair", pr)] = tmo
                        last[("lastout",)] = tmo
                        if e == 0:
                            tev = P.op("vector",
                                       lambda b, po=po, dc=dc: b.tensor_copy(
                                           moeacc[:, dc * TT:(dc + 1) * TT], po[:, :]),
                                       waits=[tmo, last.get(("macc", dc))], inc="vv")
                        else:
                            tev = P.op("vector",
                                       lambda b, po=po, dc=dc: b.tensor_add(
                                           moeacc[:, dc * TT:(dc + 1) * TT], po[:, :],
                                           moeacc[:, dc * TT:(dc + 1) * TT]),
                                       waits=[tmo, last.get(("macc", dc))], inc="vv")
                        last[("out", dc % 2)] = tev
                        last[("macc", dc)] = tev
                        if e == 2:
                            moe_ev[dc] = tev
                            spar = dc % 2
                            def fst(b, dc=dc, spar=spar):
                                b.activation(sqc2[:, spar * TT:(spar + 1) * TT],
                                             moeacc[:, dc * TT:(dc + 1) * TT], AF.Copy)
                                return b.activation(sqpp[:, spar * TT:(spar + 1) * TT],
                                                    moeacc[:, dc * TT:(dc + 1) * TT],
                                                    AF.Square)
                            tss_ = P.op("scalar", fst,
                                        waits=[tev, last.get(("sqc", spar))], inc="ss")
                            def fm1(b, dc=dc, spar=spar):
                                b.matmul(rowsC[0:1, :], onesb_s[:, 0:1],
                                         sqc2[:, spar * TT:(spar + 1) * TT],
                                         start=(dc == 0), stop=(dc == NKC - 1))
                                return b.matmul(statF2[0:1, :], onesb_s[:, 0:1],
                                                sqpp[:, spar * TT:(spar + 1) * TT],
                                                start=(dc == 0), stop=(dc == NKC - 1))
                            tm2 = P.op("tensor", fm1,
                                       waits=[tss_] +
                                             ([t_gev, last.get("rowsC_rd")] if dc == 0 else []),
                                       inc="mm")
                            last[("sqc", spar)] = tm2
                            mstat_last = tm2
                    for fc in range(NKC):
                        last[("actw", fc)] = last[("lastout",)]

                def f(b):
                    b.tensor_scalar_mul(row(R_M3), rowsC[0:1, :], 1.0 / HID)
                    b.tensor_scalar_mul(row(R_TMP), statF2[0:1, :], 1.0 / HID)
                    b.tensor_tensor(row(R_TMP2), row(R_M3), row(R_M3), ALU.mult)
                    b.tensor_sub(row(R_TMP), row(R_TMP), row(R_TMP2))
                    b.tensor_scalar_add(row(R_TMP), row(R_TMP), EPS_LN)
                    b.reciprocal(row(R_TMP), row(R_TMP))
                    return b.tensor_scalar_mul(row(R_NEGM3), row(R_M3), -1.0)
                t_r3a = P.op("vector", f, waits=[mstat_last], inc="vv")
                t_r3b = P.op("scalar",
                             lambda b: b.activation(row(R_RSTD3), row(R_TMP), AF.Sqrt),
                             waits=[t_r3a], inc="ss")
                last["rowsC_rd"] = t_r3a
                t_bc_negm3 = bcast2(row(R_NEGM3), bc0[:, :], [t_r3a])
                t_bc_rstd3 = bcast2(row(R_RSTD3), bc1[:, :], [t_r3b])

                for dc in range(NKC):
                    par = dc % 2
                    slot = dc % 2
                    sem = "xc%d" % slot
                    tl = P.op("scalar",
                              lambda b, t=t, dc=dc, slot=slot: b.dma_start(
                                  out=x1c_s[:, slot * TT:(slot + 1) * TT],
                                  in_=d_x1s[t][:, dc * TT:(dc + 1) * TT]),
                              waits=[last.get(("x1cl", slot))] +
                                    (joinA if t == 0 and dc < 2 else []),
                              inc=sem, amt=16)
                    t_l = P.tok(sem)
                    t1 = P.op("vector",
                              lambda b, dc=dc, par=par: b.tensor_add(
                                  tmpa[:, par * TT:(par + 1) * TT],
                                  moeacc[:, dc * TT:(dc + 1) * TT], bc0[:, :]),
                              waits=[t_bc_negm3, moe_ev[dc], last.get(("tmpaC", par))],
                              inc="vv")
                    last[("macc", dc)] = t1
                    t2 = P.op("vector",
                              lambda b, par=par: b.tensor_mul(
                                  tmpa[:, par * TT:(par + 1) * TT],
                                  tmpa[:, par * TT:(par + 1) * TT], bc1[:, :]),
                              waits=[t1, t_bc_rstd3], inc="vv")
                    t3 = P.op("vector",
                              lambda b, par=par, slot=slot: b.tensor_add(
                                  tmpa[:, par * TT:(par + 1) * TT],
                                  tmpa[:, par * TT:(par + 1) * TT],
                                  x1c_s[:, slot * TT:(slot + 1) * TT]),
                              waits=[t_l], inc="vv")
                    last[("tmpaC", par)] = t3
                    last[("x1cl", slot)] = t3
                    pp = tpo0 if dc % 2 == 0 else tpo1
                    def ftp(b, pp=pp, par=par):
                        for tc in range(4):
                            i = b.transpose(pp[:, tc * 128:(tc + 1) * 128],
                                            tmpa[:, par * TT + tc * 128: par * TT + (tc + 1) * 128],
                                            ident_s[:, :])
                        return i
                    tmm = P.op("tensor", ftp,
                               waits=[t3, last.get(("tpoev", dc % 2))] +
                                     ([last.get("bcfree")] if dc % 2 == 1 else []),
                               inc="mm")
                    tev = P.op("vector",
                               lambda b, pp=pp: b.tensor_copy(x2t_s[:, :], pp[:, :]),
                               waits=[tmm, last.get("x2t")], inc="vv")
                    last[("tpoev", dc % 2)] = tev
                    tso = P.op("scalar",
                               lambda b, t=t, dc=dc: b.dma_start(
                                   out=d_out[t * TT:(t + 1) * TT, dc * 128:(dc + 1) * 128]
                                   .rearrange("(tc p) d -> tc p d", p=128).transpose([1, 0, 2]),
                                   in_=x2t_s[:, :].rearrange("p (tc d) -> p tc d", tc=4)),
                               waits=[tev], inc="sout", amt=16)
                    last["x2t"] = tso
                last["hfree0"] = last[("pre", 0)]
                last["hfree1"] = last[("pre", 1)]
                last["coef3_rd"] = last[("lastout",)]
                last["rows2f"] = t_gmvt
                last["rowsL_rd"] = last[("lastout",)]

            if DEBUG_TAPS:
                def ftap3(b):
                    r = []
                    r.append((b.dma_start(out=d_moedbg, in_=moeacc[:, :]), 16))
                    r.append((b.dma_start(out=d_rowdbg2, in_=rows[0:1, :]), 16))
                    r.append((b.dma_start(out=d_cfdbg, in_=coefb[:, :]), 16))
                    return r
                P.op("gpsimd", ftap3,
                     waits=[P.tok("vv"), P.tok("ss"), P.tok("mm")], inc="smis", amt=48)
            P.op("sync", lambda b: b.nop(), waits=[P.tok("sout"), P.tok("smis")])
            P.flush(blk, sems)

    return nc


_NC_CACHE = None


def kernel(**inputs):
    global _NC_CACHE
    in_maps = host_prep(inputs)
    if _NC_CACHE is None:
        _NC_CACHE = build_nc()
    nc = _NC_CACHE
    res = run_bass_kernel_spmd(nc, in_maps, core_ids=list(range(NCORES)))
    out = np.concatenate([res.results[c]["out"] for c in range(NCORES)], axis=0)
    return out.reshape(2, S_FULL, HID)
